# version 42
# baseline (speedup 1.0000x reference)
import contextlib
import numpy as np
import ml_dtypes

import concourse.bass as bass
import concourse.mybir as mybir
from concourse.bass_utils import run_bass_kernel_spmd

F32 = mybir.dt.float32
BF16 = mybir.dt.bfloat16
AF = mybir.ActivationFunctionType
ALU = mybir.AluOpType

B, S, D = 4, 2048, 2048
H, DH = 16, 128
I = 5632
NDC = D // 128
NIC = I // 128
T = 1024
ISQ = float(1.0 / np.sqrt(DH))
EPS = 1e-5
RN = float(1.0 / D)

O_ONES = 0
O_GS = 64
O_US = 108
O_DS = 152
O_Z = 168
O_XB = 192
O_SQ = 2240
O_MS = 3264
O_ES = 4288
O_TMP = 5312
O_AREP = 6336
O_BREP = 6848
O_WZ = 7360
O_V = 19648
O_Q = 27840
O_K = 36032
O_H = 19648
O_XH = 44224
POOL_N = 52416

SEM_STEP = dict(W=16, W2=16, W3=16, W4=16, W5=16, W6=16, S=16, X=16, U=16, R=16, Y=16, Y2=16, Y3=16, Y4=16, O=16, O2=16, O3=16, O4=16, XO=16, XO2=16, XO3=16, XO4=16, T=1, A=1, V=1, C=1)


class Bld:

    def __init__(self):
        self.mode = "count"
        self.cnt = {k: 0 for k in SEM_STEP}
        self.ev = {}
        self.sems = None

    def op(self, fn, inc=None, waits=()):
        if self.mode == "count":
            if inc is not None:
                sem = inc[0]
                self.cnt[sem] += SEM_STEP[sem]
                key = (sem,) + tuple(inc[1])
                assert key not in self.ev, f"dup event {key}"
                self.ev[key] = self.cnt[sem]
            return None
        ins = fn()
        if inc is not None:
            ins.then_inc(self.sems[inc[0]], SEM_STEP[inc[0]])
        return ins

    def wait(self, eng, sem, tag):
        if self.mode == "count":
            return
        key = (sem,) + tuple(tag)
        eng.wait_ge(self.sems[sem], self.ev[key])

    def wait_maybe(self, eng, sem, tag):
        if self.mode == "count":
            return
        key = (sem,) + tuple(tag)
        if key in self.ev:
            eng.wait_ge(self.sems[sem], self.ev[key])

    def drain(self, eng):
        if self.mode == "emit":
            eng.drain()


def build(debug=False):
    nc = bass.Bass("TRN2", target_bir_lowering=False)

    xT = nc.dram_tensor("xT", [D, S], F32, kind="ExternalInput").ap()
    WQ = nc.dram_tensor("WQ", [8, 128, NDC * 128], F32, kind="ExternalInput").ap()
    WK = nc.dram_tensor("WK", [8, 128, NDC * 128], F32, kind="ExternalInput").ap()
    WV = nc.dram_tensor("WV", [16, 128, NDC * 128], F32, kind="ExternalInput").ap()
    WO = nc.dram_tensor("WOw", [16, 128, NDC * 128], F32, kind="ExternalInput").ap()
    GW = nc.dram_tensor("GW", [NIC, 128, NDC * 128], F32, kind="ExternalInput").ap()
    UW = nc.dram_tensor("UW", [NIC, 128, NDC * 128], F32, kind="ExternalInput").ap()
    DW = nc.dram_tensor("DW", [NDC, 128, NIC * 128], F32, kind="ExternalInput").ap()
    GSs = nc.dram_tensor("GSs", [128, NIC], F32, kind="ExternalInput").ap()
    USs = nc.dram_tensor("USs", [128, NIC], F32, kind="ExternalInput").ap()
    DSs = nc.dram_tensor("DSs", [128, NDC], F32, kind="ExternalInput").ap()

    outT = nc.dram_tensor("outT", [D, T], F32, kind="ExternalOutput").ap()

    x1b = nc.dram_tensor("x1b", [D, T], F32).ap()
    cs_own = nc.dram_tensor("cs_own", [8, S], F32).ap()
    cs_part = nc.dram_tensor("cs_part", [8, T], F32).ap()
    cc_in = nc.dram_tensor("cc_in", [8, T], F32).ap()
    cc_sum = nc.dram_tensor("cc_sum", [8, T], F32).ap()

    dbg = {}
    if debug:
        dbg["q"] = nc.dram_tensor("dbg_q", [128, 8 * 2048], BF16, kind="ExternalOutput").ap()
        dbg["v"] = nc.dram_tensor("dbg_v", [128, 16 * 1024], BF16, kind="ExternalOutput").ap()
        dbg["cs"] = nc.dram_tensor("dbg_cs", [8, S], F32, kind="ExternalOutput").ap()
        dbg["csp"] = nc.dram_tensor("dbg_csp", [8, T], F32, kind="ExternalOutput").ap()
        dbg["x1"] = nc.dram_tensor("dbg_x1", [D, T], F32, kind="ExternalOutput").ap()

    Bd = Bld()
    RBN = 6
    RB_SEM = ["Y", "Y2", "W", "W2", "W5", "W6"]
    LXD = NDC if debug else RBN
    YSEM = ["Y", "Y2", "Y3", "Y4"]
    XOSEM = ["XO", "XO2", "XO3", "XO4"]
    LX1D = {i: [oc for oc in range(LXD) if oc % 4 == i] for i in range(4)}
    LXO = {i: [oc for oc in range((0 if debug else RBN), NDC) if oc % 4 == i]
           for i in range(4)}
    FI_SEQ = []
    for _k in range(RBN):
        FI_SEQ += [RBN + _k, _k]
    FI_SEQ += list(range(2 * RBN, NDC))
    FI_SLOT = {oc: i % 4 for i, oc in enumerate(FI_SEQ)}

    with contextlib.ExitStack() as ctx:
        pool = ctx.enter_context(nc.sbuf_tensor("pool", [128, POOL_N], F32))

        def f32(off, n):
            return pool[:, off:off + n]

        def bf(off, nf32):
            return pool[:, off:off + nf32].bitcast(BF16)

        ones = bf(O_ONES, 64)
        gs_t = f32(O_GS, NIC)
        us_t = f32(O_US, NIC)
        ds_t = f32(O_DS, NDC)
        z1 = [f32(O_Z + i, 1) for i in range(8)]
        zf = f32(O_Z + 8, 1)
        rt = f32(O_Z + 9, 1)
        rbv = bf(O_Z + 10, 4)
        rb = [rbv[:, 2 * i:2 * i + 1] for i in range(4)]
        eps_t = f32(O_Z + 14, 1)
        xb = [f32(O_XB, 1024), f32(O_XB + 1024, 1024)]
        sq = [bf(O_SQ, 512), bf(O_SQ + 512, 512)]
        ms = f32(O_MS, 1024)
        es = f32(O_ES, 1024)
        tmp1 = f32(O_TMP, 1024)
        arep = bf(O_AREP, 512)
        brep = bf(O_BREP, 512)
        wfb = [f32(O_WZ, 2048), f32(O_WZ + 2048, 2048)]
        wbb = [bf(O_WZ + 4096 + 1024 * i, 1024) for i in range(4)]
        gwf = wfb
        gwb = wbb
        uwf = [f32(O_WZ + 6144, 2048), f32(O_WZ + 8192, 2048)]
        uwb = [bf(O_WZ + 10240, 1024), bf(O_WZ + 11264, 1024)]
        dwf = f32(O_WZ, 5632)
        dwf_pre = f32(O_XB, 5632)
        dwb = [bf(O_WZ + 5632, 2816), bf(O_WZ + 8448, 2816)]
        Vt = bf(O_V, 8192)
        Qt = bf(O_Q, 8192)
        Kt = bf(O_K, 8192)
        attn_bf = bf(O_Q, 8192)
        csb_at = [f32(O_TMP, 1024), f32(O_TMP + 1024, 1024)]
        csb = [f32(O_K + 1024 * i, 1024) for i in range(4)]
        cci_sb = pool[0:8, O_K:O_K + 1024]
        ccs_sb = pool[0:8, O_K + 1024:O_K + 2048]
        sub_sb = pool[0:8, O_K + 2048:O_K + 3072]
        cci_h2 = pool[32:36, O_K:O_K + 1024]
        ccs_h2 = pool[32:36, O_K + 1024:O_K + 2048]
        sub_h2 = pool[32:36, O_K + 2048:O_K + 3072]
        x1f = [f32(O_K + 4096, 1024), f32(O_K + 5120, 1024),
               f32(O_K, 1024), f32(O_K + 1024, 1024)]
        ostage = [f32(O_K + 2048 + 1024 * i, 1024) for i in range(4)]
        xrb = [f32(O_XB, 1024), f32(O_XB + 1024, 1024)] + \
              [f32(O_WZ + 1024 * i, 1024) for i in range(4)]
        gtmp = [bf(O_K + 6144, 512), bf(O_K + 6656, 512)]
        utmp = [bf(O_K + 7168, 512), bf(O_K + 7680, 512)]
        xh = bf(O_XH, 8192)
        Et = [bf(O_WZ + 8192 + 1024 * i, 1024) for i in range(4)]
        xBb = [bf(O_WZ + 8192, 512), bf(O_WZ + 8704, 512)]
        sqB = [bf(O_WZ + 9216, 512), bf(O_WZ + 9728, 512)]
        cstage = pool[0:33, O_MS:O_MS + 2048]
        x1bf = bf(O_XH, 8192)
        ffn_bf = x1bf
        ht = bf(O_H, 22528)

        ps = ctx.enter_context(nc.psum_tensor("ps", [128, 4096], F32))

        sems = {}
        for name in SEM_STEP:
            sems[name] = ctx.enter_context(nc.semaphore("s" + name))
        Bd.sems = sems


        def ph_setup(e):
            def f_sync(s):
                Bd.op(lambda: s.dma_start(out=gs_t, in_=GSs), inc=("S", ("gsl",)))
                Bd.op(lambda: s.dma_start(out=us_t, in_=USs), inc=("S", ("usl",)))
                Bd.op(lambda: s.dma_start(out=ds_t, in_=DSs), inc=("S", ("dsl",)))
            def f_vec(v):
                Bd.op(lambda: v.memset(eps_t, EPS))
                Bd.drain(v)
                Bd.op(lambda: v.memset(ones[:, :], 1.0), inc=("V", ("ones",)))
            return dict(sync=f_sync, vector=f_vec)

        def ph_lnqkv(e, pa):
            coloff = pa * 1024
            fcs = []
            if pa == 0:
                for i in range(8):
                    fcs.append(("q", WQ, i))
                for i in range(8):
                    fcs.append(("k", WK, i))
                for i in range(16):
                    fcs.append(("v", WV, i))
            else:
                for i in range(8):
                    fcs.append(("q", WQ, i))
                    fcs.append(("k", WK, i))
            nfc = len(fcs)

            def f_gp(g):
                for dc in range(2 if pa == 1 else 0, NDC):
                    if dc >= 2:
                        Bd.wait(g, "V", ("lncast", pa, dc - 2))
                    Bd.op(lambda dc=dc: g.dma_start(
                        out=xb[dc % 2],
                        in_=xT[dc * 128:(dc + 1) * 128, coloff:coloff + 1024]),
                        inc=("X" if dc % 2 == 0 else "U", ("xld", pa, dc)))
                if pa == 0:
                    for dc in range(NDC):
                        Bd.wait(g, "V", ("lncast", 0, 14 + dc) if dc < 2
                                else ("xcB", dc - 2))
                        Bd.op(lambda dc=dc: g.dma_start(
                            out=xb[dc % 2],
                            in_=xT[dc * 128:(dc + 1) * 128, 1024:2048]),
                            inc=("X" if dc % 2 == 0 else "U", ("xldB", dc)))
                    Bd.wait(g, "V", ("xcB", NDC - 2))
                    Bd.op(lambda: g.dma_start(
                        out=xb[0], in_=xT[0:128, 1024:2048]),
                        inc=("X", ("xld", 1, 0)))
                    Bd.wait(g, "V", ("xcB", NDC - 1))
                    Bd.op(lambda: g.dma_start(
                        out=xb[1], in_=xT[128:256, 1024:2048]),
                        inc=("U", ("xld", 1, 1)))

            def f_vec(v):
                if pa == 1:
                    Bd.wait(v, "T", ("qkv", 0, 31))
                    def wc(fc):
                        Bd.wait(v, "W" if fc % 2 == 0 else "W2", ("wld", 1, fc))
                        Bd.op(lambda fc=fc: v.tensor_copy(wbb[fc % 4][:, :], wfb[fc % 2]),
                              inc=("V", ("wcast", 1, fc)))
                    wc(0)
                    wc(1)
                    for dc in range(NDC):
                        Bd.wait(v, "X" if dc % 2 == 0 else "U", ("xld", 1, dc))
                        sl = xh[:, dc * 1024:(dc + 1) * 1024]
                        Bd.op(lambda sl=sl, dc=dc: v.tensor_mul(sl, xb[dc % 2], arep),
                              inc=("V", ("lncast", 1, dc)))
                        Bd.drain(v)
                        Bd.op(lambda sl=sl: v.tensor_add(sl, sl, brep),
                              inc=("V", ("norm", 1, dc)))
                        if dc == 2:
                            wc(2)
                        if dc == 4:
                            wc(3)
                    return
                for dc in range(NDC):
                    Bd.wait(v, "X" if dc % 2 == 0 else "U", ("xld", pa, dc))
                    Bd.op(lambda dc=dc: v.tensor_copy(
                        xh[:, dc * 1024:(dc + 1) * 1024], xb[dc % 2]),
                        inc=("V", ("lncast", pa, dc)))
                Bd.wait(v, "A", ("es", pa))
                Bd.op(lambda: v.tensor_mul(tmp1, ms, ms))
                Bd.drain(v)
                Bd.op(lambda: v.tensor_sub(es, es, tmp1), inc=("V", ("sub", pa)))
                Bd.wait(v, "A", ("sqr", pa))
                Bd.op(lambda: v.scalar_tensor_tensor(
                    brep, ms, -1.0, arep, ALU.mult, ALU.mult),
                    inc=("V", ("brep", pa)))
                for fc in range(min(4, nfc)):
                    Bd.wait(v, "W" if fc % 2 == 0 else "W2", ("wld", pa, fc))
                    Bd.op(lambda fc=fc: v.tensor_copy(wbb[fc % 4][:, :], wfb[fc % 2]),
                          inc=("V", ("wcast", pa, fc)))
                Bd.wait(v, "T", ("stat", pa, NDC - 1))
                for dc in range(NDC):
                    sl = xh[:, dc * 1024:(dc + 1) * 1024]
                    Bd.op(lambda sl=sl: v.tensor_mul(sl, sl, arep))
                    Bd.drain(v)
                    Bd.op(lambda sl=sl: v.tensor_add(sl, sl, brep),
                          inc=("V", ("norm", pa, dc)))
                if pa == 0:
                    for fc in range(4, nfc):
                        Bd.wait(v, "W" if fc % 2 == 0 else "W2", ("wld", pa, fc))
                        Bd.wait(v, "T", ("qkv", pa, fc - 4))
                        Bd.op(lambda fc=fc: v.tensor_copy(wbb[fc % 4][:, :], wfb[fc % 2]),
                              inc=("V", ("wcast", pa, fc)))
                        dcB = fc - 4
                        if dcB < NDC:
                            Bd.wait(v, "X" if dcB % 2 == 0 else "U", ("xldB", dcB))
                            if dcB >= 2:
                                Bd.wait(v, "A", ("sqB", dcB - 2))
                                Bd.wait(v, "T", ("statB", dcB - 2))
                            Bd.op(lambda dcB=dcB: v.tensor_copy(
                                xBb[dcB % 2][:, :], xb[dcB % 2]),
                                inc=("V", ("xcB", dcB)))
                    Bd.wait(v, "A", ("es", 1))
                    Bd.op(lambda: v.tensor_mul(tmp1, ms, ms))
                    Bd.drain(v)
                    Bd.op(lambda: v.tensor_sub(es, es, tmp1), inc=("V", ("sub", 1)))
                    Bd.wait(v, "A", ("sqr", 1))
                    Bd.op(lambda: v.scalar_tensor_tensor(
                        brep, ms, -1.0, arep, ALU.mult, ALU.mult),
                        inc=("V", ("brep", 1)))

            def f_sc(s):
                if pa == 1:
                    return
                for dc in range(NDC):
                    Bd.wait(s, "V", ("lncast", pa, dc))
                    if dc >= 2:
                        Bd.wait(s, "T", ("stat", pa, dc - 2))
                    Bd.op(lambda dc=dc: s.activation(
                        sq[dc % 2][:, :], xh[:, dc * 1024:(dc + 1) * 1024], AF.Square),
                        inc=("A", ("sq", pa, dc)))
                Bd.wait(s, "T", ("stat", pa, NDC - 1))
                Bd.op(lambda: s.activation(ms, ps[:, 0:1024], AF.Copy, scale=RN),
                      inc=("A", ("ms", pa)))
                Bd.op(lambda: s.activation(es, ps[:, 1024:2048], AF.Copy, scale=RN),
                      inc=("A", ("es", pa)))
                Bd.wait(s, "V", ("sub", pa))
                Bd.op(lambda: s.activation(tmp1, es, AF.Ln, bias=eps_t))
                Bd.drain(s)
                Bd.op(lambda: s.activation(arep, tmp1, AF.Exp, scale=-0.5),
                      inc=("A", ("sqr", pa)))
                if pa == 0:
                    for fc in range(nfc):
                        kind, _, i = fcs[fc]
                        if kind == "q":
                            dst = Qt[:, i * 2048 + coloff:i * 2048 + coloff + 1024]
                        elif kind == "k":
                            dst = Kt[:, i * 2048 + coloff:i * 2048 + coloff + 1024]
                        else:
                            dst = Vt[:, i * 1024:(i + 1) * 1024]
                        Bd.wait(s, "T", ("qkv", pa, fc))
                        po = fc * 1024 if fc < 4 else (fc % 2) * 1024
                        Bd.op(lambda dst=dst, po=po: s.activation(
                            dst, ps[:, po:po + 1024], AF.Copy),
                            inc=("A", ("ev", pa, fc)))
                        dcB = fc - 2
                        if 0 <= dcB < NDC:
                            Bd.wait(s, "V", ("xcB", dcB))
                            if dcB >= 2:
                                Bd.wait(s, "T", ("statB", dcB - 2))
                            Bd.op(lambda dcB=dcB: s.activation(
                                sqB[dcB % 2][:, :], xBb[dcB % 2][:, :], AF.Square),
                                inc=("A", ("sqB", dcB)))
                        if fc == 20:
                            Bd.wait(s, "T", ("statB", NDC - 1))
                            Bd.op(lambda: s.activation(
                                ms, ps[:, 2048:3072], AF.Copy, scale=RN),
                                inc=("A", ("ms", 1)))
                            Bd.op(lambda: s.activation(
                                es, ps[:, 3072:4096], AF.Copy, scale=RN),
                                inc=("A", ("es", 1)))
                    Bd.wait(s, "V", ("sub", 1))
                    Bd.op(lambda: s.activation(tmp1, es, AF.Ln, bias=eps_t))
                    Bd.drain(s)
                    Bd.op(lambda: s.activation(arep, tmp1, AF.Exp, scale=-0.5),
                          inc=("A", ("sqr", 1)))

            def f_te(t):
                if pa == 1:
                    return
                Bd.wait(t, "V", ("ones",))
                for dc in range(NDC):
                    Bd.wait(t, "V", ("lncast", pa, dc))
                    Bd.wait(t, "A", ("sq", pa, dc))
                    st = (dc == 0)
                    sp = (dc == NDC - 1)
                    for pc in range(2):
                        Bd.op(lambda dc=dc, pc=pc, st=st, sp=sp: t.matmul(
                            ps[:, pc * 512:(pc + 1) * 512], ones[:, :],
                            xh[:, dc * 1024 + pc * 512:dc * 1024 + (pc + 1) * 512],
                            start=st, stop=sp, skip_group_check=True))
                    for pc in range(2):
                        Bd.op(lambda dc=dc, pc=pc, st=st, sp=sp: t.matmul(
                            ps[:, 1024 + pc * 512:1024 + (pc + 1) * 512], ones[:, :],
                            sq[dc % 2][:, pc * 512:(pc + 1) * 512],
                            start=st, stop=sp, skip_group_check=True),
                            inc=("T", ("stat", pa, dc)) if (pc == 1) else None)
                if pa == 0:
                    def statB_mm(dcB):
                        Bd.wait(t, "V", ("xcB", dcB))
                        Bd.wait(t, "A", ("sqB", dcB))
                        if dcB == 0:
                            Bd.wait(t, "A", ("ev", pa, 2))
                            Bd.wait(t, "A", ("ev", pa, 3))
                        st = (dcB == 0)
                        sp = (dcB == NDC - 1)
                        for pc in range(2):
                            Bd.op(lambda dcB=dcB, pc=pc, st=st, sp=sp: t.matmul(
                                ps[:, 2048 + pc * 512:2048 + (pc + 1) * 512],
                                ones[:, :],
                                xBb[dcB % 2][:, pc * 512:(pc + 1) * 512],
                                start=st, stop=sp, skip_group_check=True))
                        for pc in range(2):
                            Bd.op(lambda dcB=dcB, pc=pc, st=st, sp=sp: t.matmul(
                                ps[:, 3072 + pc * 512:3072 + (pc + 1) * 512],
                                ones[:, :],
                                sqB[dcB % 2][:, pc * 512:(pc + 1) * 512],
                                start=st, stop=sp, skip_group_check=True),
                                inc=("T", ("statB", dcB)) if pc == 1 else None)

                    for fc in range(4):
                        Bd.wait(t, "V", ("wcast", pa, fc))
                    Bd.wait(t, "A", ("es", pa))
                    for dc in range(NDC):
                        Bd.wait(t, "V", ("norm", pa, dc))
                        st = (dc == 0)
                        sp = (dc == NDC - 1)
                        for fc in range(4):
                            for pc in range(2):
                                Bd.op(lambda fc=fc, dc=dc, pc=pc, st=st, sp=sp: t.matmul(
                                    ps[:, fc * 1024 + pc * 512:fc * 1024 + (pc + 1) * 512],
                                    wbb[fc][:, dc * 128:(dc + 1) * 128],
                                    xh[:, dc * 1024 + pc * 512:dc * 1024 + (pc + 1) * 512],
                                    start=st, stop=sp, skip_group_check=True),
                                    inc=("T", ("qkv", pa, fc)) if (sp and pc == 1) else None)
                    for fc in range(4, nfc):
                        Bd.wait(t, "V", ("wcast", pa, fc))
                        if 4 <= fc < 20:
                            statB_mm(fc - 4)
                        Bd.wait(t, "A", ("ev", pa, fc - 4 if fc < 6 else fc - 2))
                        for dc in range(NDC):
                            st = (dc == 0)
                            sp = (dc == NDC - 1)
                            for pc in range(2):
                                Bd.op(lambda fc=fc, dc=dc, pc=pc, st=st, sp=sp: t.matmul(
                                    ps[:, (fc % 2) * 1024 + pc * 512:(fc % 2) * 1024 + (pc + 1) * 512],
                                    wbb[fc % 4][:, dc * 128:(dc + 1) * 128],
                                    xh[:, dc * 1024 + pc * 512:dc * 1024 + (pc + 1) * 512],
                                    start=st, stop=sp, skip_group_check=True),
                                    inc=("T", ("qkv", pa, fc)) if (sp and pc == 1) else None)

            def f_sync(s):
                for fc in range(nfc):
                    _, wap, i = fcs[fc]
                    if fc >= 2:
                        Bd.wait(s, "V", ("wcast", pa, fc - 2))
                    elif pa == 1:
                        Bd.wait(s, "V", ("wcast", 0, 30 + fc))
                    Bd.op(lambda wap=wap, i=i, fc=fc: s.dma_start(
                        out=wfb[fc % 2], in_=wap[i]),
                        inc=("W" if fc % 2 == 0 else "W2", ("wld", pa, fc)))
                if pa == 1:
                    Bd.wait(s, "V", ("wcast", 1, 14))
                    Bd.op(lambda: s.dma_start(out=wfb[0], in_=WO[0]),
                          inc=("W", ("wol", 0)))
                    Bd.wait(s, "V", ("wcast", 1, 15))
                    Bd.op(lambda: s.dma_start(out=wfb[1], in_=WO[1]),
                          inc=("W2", ("wol", 1)))

            return dict(sync=f_sync, gpsimd=f_gp, scalar=f_sc, vector=f_vec, tensor=f_te)

        def ph_att(e):
            CS_SLOT = [(0, 3072), (32, 3072), (0, 3584), (32, 3584)]

            def colsum_mm(t, h, j):
                Bd.wait(t, "V", ("rb", h, j))
                Bd.wait(t, "A", ("exp", h, j, 1))
                if j == 0:
                    Bd.wait_maybe(t, "A", ("csev", h - 1))
                    if h == 0:
                        Bd.wait(t, "A", ("ev", 1, 1))
                for pc in range(4):
                    row, col = CS_SLOT[pc]
                    tp = (0, 32) if row == 32 else None
                    Bd.op(lambda h=h, j=j, pc=pc, row=row, col=col, tp=tp: t.matmul(
                        ps[row:row + 1, col:col + 512],
                        rb[j % 4], Et[j % 4][:, pc * 512:(pc + 1) * 512],
                        start=(j == 0), stop=(j == 15), tile_position=tp,
                        skip_group_check=True),
                        inc=("T", ("cs", h, j)) if pc == 3 else None)

            STEPS = [(h, qc) for h in range(8) for qc in range(16)]

            def qkvb_mm(t, fc, dcs, first):
                if first:
                    Bd.wait(t, "V", ("wcast", 1, fc))
                    if fc == 0:
                        Bd.wait(t, "A", ("es", 1))
                    if fc >= 1:
                        Bd.wait(t, "A", ("ev", 1, fc - 1))
                for dc in dcs:
                    if fc == 0:
                        Bd.wait(t, "V", ("norm", 1, dc))
                    st = (dc == 0)
                    sp = (dc == NDC - 1)
                    for pc in range(2):
                        Bd.op(lambda fc=fc, dc=dc, pc=pc, st=st, sp=sp: t.matmul(
                            ps[:, 2048 + pc * 512:2048 + (pc + 1) * 512],
                            wbb[fc % 4][:, dc * 128:(dc + 1) * 128],
                            xh[:, dc * 1024 + pc * 512:dc * 1024 + (pc + 1) * 512],
                            start=st, stop=sp, skip_group_check=True),
                            inc=("T", ("qkv", 1, fc)) if (sp and pc == 1) else None)

            def f_te(t):
                Bd.wait(t, "V", ("wcast", 1, 0))
                Bd.wait(t, "V", ("wcast", 1, 1))
                Bd.wait(t, "A", ("es", 1))
                for dc in range(NDC):
                    Bd.wait(t, "V", ("norm", 1, dc))
                    st = (dc == 0)
                    sp = (dc == NDC - 1)
                    for fc in range(2):
                        for pc in range(2):
                            Bd.op(lambda fc=fc, dc=dc, pc=pc, st=st, sp=sp: t.matmul(
                                ps[:, 2048 + fc * 1024 + pc * 512:2048 + fc * 1024 + (pc + 1) * 512],
                                wbb[fc][:, dc * 128:(dc + 1) * 128],
                                xh[:, dc * 1024 + pc * 512:dc * 1024 + (pc + 1) * 512],
                                start=st, stop=sp, skip_group_check=True),
                                inc=("T", ("qkv", 1, fc)) if (sp and pc == 1) else None)
                for s_i, (h, qc) in enumerate(STEPS):
                    if s_i == 0:
                        Bd.wait(t, "A", ("es", 1))
                        Bd.wait(t, "A", ("ev", 0, 30))
                        Bd.wait(t, "A", ("ev", 0, 31))
                        Bd.wait(t, "A", ("ev", 1, 1))
                    elif qc == 0:
                        Bd.wait(t, "A", ("ev", 1, 2 * h + 1))
                    for hf in range(2):
                        u = 2 * s_i + hf
                        if u >= 2:
                            up = u - 2
                            hp, qp = STEPS[up // 2]
                            Bd.wait(t, "A", ("exp", hp, qp, up % 2))
                        po = (u % 2) * 1024
                        for pc in range(2):
                            Bd.op(lambda h=h, qc=qc, hf=hf, pc=pc, po=po: t.matmul(
                                ps[:, po + pc * 512:po + (pc + 1) * 512],
                                Qt[:, h * 2048 + qc * 128:h * 2048 + (qc + 1) * 128],
                                Kt[:, h * 2048 + hf * 1024 + pc * 512:h * 2048 + hf * 1024 + (pc + 1) * 512],
                                start=True, stop=True, skip_group_check=True),
                                inc=("T", ("sc", h, qc, hf)) if pc == 1 else None)
                    if s_i >= 2:
                        colsum_mm(t, *STEPS[s_i - 2])
                    if h <= 6:
                        fc = 2 * h + 2 + (0 if qc < 8 else 1)
                        dc0 = 2 * (qc % 8)
                        qkvb_mm(t, fc, (dc0, dc0 + 1), dc0 == 0)
                colsum_mm(t, *STEPS[126])
                colsum_mm(t, *STEPS[127])

            def cs_evac(s, hd):
                Bd.wait(s, "T", ("cs", hd, 15))
                if hd >= 1:
                    Bd.wait(s, "X", ("csd", hd - 1, 5))
                else:
                    Bd.wait(s, "V", ("brep", 1))
                for pc in range(4):
                    row, col = CS_SLOT[pc]
                    Bd.op(lambda pc=pc, row=row, col=col: s.activation(
                        cstage[row:row + 1, pc * 512:(pc + 1) * 512],
                        ps[row:row + 1, col:col + 512], AF.Copy),
                        inc=("A", ("csev", hd)) if pc == 3 else None)

            def qkvb_ev(s, fc):
                i = fc // 2
                dst = (Qt if fc % 2 == 0 else Kt)[:, i * 2048 + 1024:i * 2048 + 2048]
                Bd.wait(s, "T", ("qkv", 1, fc))
                po = 3072 if fc == 1 else 2048
                Bd.op(lambda dst=dst, po=po: s.activation(
                    dst, ps[:, po:po + 1024], AF.Copy),
                    inc=("A", ("ev", 1, fc)))

            def f_sc(s):
                qkvb_ev(s, 0)
                qkvb_ev(s, 1)
                for s_i, (h, qc) in enumerate(STEPS):
                    if qc == 0 and h >= 1:
                        qkvb_ev(s, 2 * h + 1)
                    if qc == 8 and h <= 6:
                        qkvb_ev(s, 2 * h + 2)
                    for hf in range(2):
                        u = 2 * s_i + hf
                        Bd.wait(s, "T", ("sc", h, qc, hf))
                        if hf == 0 and s_i >= 4:
                            Bd.wait(s, "V", ("rb", *STEPS[s_i - 4]))
                            Bd.wait(s, "T", ("cs", *STEPS[s_i - 4]))
                        zdst = z1[u % 8]
                        po = (u % 2) * 1024
                        Bd.op(lambda h=h, qc=qc, hf=hf, zdst=zdst, po=po: s.activation(
                            Et[(h * 16 + qc) % 4][:, hf * 1024:(hf + 1) * 1024],
                            ps[:, po:po + 1024],
                            AF.Exp, scale=ISQ, accum_out=zdst),
                            inc=("A", ("exp", h, qc, hf)))
                    if s_i >= 2 and STEPS[s_i - 2][1] == 15:
                        cs_evac(s, STEPS[s_i - 2][0])
                cs_evac(s, 7)
                Bd.wait(s, "W", ("wol", 0))
                Bd.wait(s, "T", ("qkv", 1, 12))
                Bd.op(lambda: s.activation(wbb[0][:, :], wfb[0], AF.Copy),
                      inc=("A", ("wosc", 0)))
                Bd.wait(s, "W2", ("wol", 1))
                Bd.wait(s, "T", ("qkv", 1, 13))
                Bd.op(lambda: s.activation(wbb[1][:, :], wfb[1], AF.Copy),
                      inc=("A", ("wosc", 1)))

            def f_vec(v):
                for s_i, (h, qc) in enumerate(STEPS):
                    Bd.wait(v, "A", ("exp", h, qc, 1))
                    z0 = z1[(2 * s_i) % 8]
                    z1b = z1[(2 * s_i + 1) % 8]
                    Bd.op(lambda z0=z0, z1b=z1b: v.tensor_add(zf, z0, z1b))
                    Bd.drain(v)
                    Bd.op(lambda: v.reciprocal(rt, zf))
                    Bd.drain(v)
                    p = s_i % 4
                    Bd.op(lambda p=p: v.tensor_copy(rb[p], rt),
                          inc=("V", ("rb", h, qc)))
                    if qc == 2 and h <= 5:
                        for fc in (2 * h + 4, 2 * h + 5):
                            Bd.wait(v, "W" if fc % 2 == 0 else "W2", ("wld", 1, fc))
                            Bd.wait(v, "T", ("qkv", 1, fc - 4))
                            Bd.op(lambda fc=fc: v.tensor_copy(
                                wbb[fc % 4][:, :], wfb[fc % 2]),
                                inc=("V", ("wcast", 1, fc)))
                    if s_i >= 19 and s_i % 16 == 3:
                        hd = (s_i - 19) // 16
                        Bd.wait(v, "R", ("csb", hd))
                        sl = Vt[:, hd * 1024:(hd + 1) * 1024]
                        Bd.op(lambda sl=sl, hd=hd: v.tensor_mul(sl, sl, csb_at[hd % 2]),
                              inc=("V", ("att", hd)))
                Bd.wait(v, "R", ("csb", 7))
                sl7 = Vt[:, 7 * 1024:8 * 1024]
                Bd.op(lambda: v.tensor_mul(sl7, sl7, csb_at[7 % 2]),
                      inc=("V", ("att", 7)))
                Bd.wait(v, "X", ("cci1",))
                Bd.op(lambda: v.tensor_sub(sub_sb[0:4, :], ccs_sb[0:4, :],
                                           cci_sb[0:4, :]),
                      inc=("V", ("ccsub1",)))
                for l in range(8, 12):
                    Bd.wait(v, "R", ("csb", l))
                    sl = Vt[:, l * 1024:(l + 1) * 1024]
                    Bd.op(lambda sl=sl, l=l: v.tensor_mul(sl, sl, csb[l % 4]),
                          inc=("V", ("att", l)))

            def f_gp(g):
                CSD_SRC = [(0, 0), (32, 512), (0, 1024), (32, 1536)]
                for h in range(8):
                    Bd.wait(g, "A", ("csev", h))
                    for i, (row, col) in enumerate(CSD_SRC):
                        Bd.op(lambda h=h, row=row, col=col: g.dma_start(
                            out=cs_own[h:h + 1, col:col + 512],
                            in_=cstage[row:row + 1, col:col + 512]),
                            inc=("X", ("csd", h, i)))
                    Bd.op(lambda h=h: g.dma_start(
                        out=cc_in[h:h + 1, 0:512], in_=cstage[0:1, 1024:1536]),
                        inc=("X", ("csd", h, 4)))
                    Bd.op(lambda h=h: g.dma_start(
                        out=cc_in[h:h + 1, 512:1024], in_=cstage[32:33, 1536:2048]),
                        inc=("X", ("csd", h, 5)))
                    Bd.wait(g, "X", ("csd", h, 5))
                    if h == 0:
                        Bd.wait(g, "V", ("norm", 1, NDC - 1))
                    if h >= 1:
                        Bd.wait(g, "V", ("att", h - 1))
                    Bd.op(lambda h=h: g.dma_start(
                        out=csb_at[h % 2],
                        in_=cs_own[h:h + 1, 0:1024].to_broadcast((128, 1024))),
                        inc=("R", ("csb", h)))
                    if h == 3:
                        Bd.wait(g, "X", ("csd", 3, 5))
                        Bd.op(lambda: g.collective_compute(
                            "AllReduce", ALU.add,
                            replica_groups=[[0, 1], [2, 3], [4, 5], [6, 7]],
                            ins=[cc_in[0:4, :].opt()], outs=[cc_sum[0:4, :].opt()]),
                            inc=("C", ("ag1",)))
                    if h == 7:
                        Bd.wait(g, "X", ("csd", 7, 5))
                        Bd.op(lambda: g.collective_compute(
                            "AllReduce", ALU.add,
                            replica_groups=[[0, 1], [2, 3], [4, 5], [6, 7]],
                            ins=[cc_in[4:8, :].opt()], outs=[cc_sum[4:8, :].opt()]),
                            inc=("C", ("ag2",)))
                Bd.wait(g, "T", ("cs", 7, 15))
                Bd.wait(g, "C", ("ag1",))
                Bd.op(lambda: g.dma_start(out=ccs_sb[0:4, :], in_=cc_sum[0:4, :]),
                      inc=("X", ("ccs1",)))
                Bd.op(lambda: g.dma_start(out=cci_sb[0:4, :], in_=cc_in[0:4, :]),
                      inc=("X", ("cci1",)))
                Bd.wait(g, "V", ("ccsub1",))
                Bd.op(lambda: g.dma_start(out=cs_part[0:4, :], in_=sub_sb[0:4, :]),
                      inc=("X", ("csp1",)))
                for l in range(8, 12):
                    Bd.wait(g, "X", ("csp1",))
                    Bd.wait(g, "V", ("att", l - 1))
                    Bd.op(lambda l=l: g.dma_start(
                        out=csb[l % 4],
                        in_=cs_part[l - 8:l - 7, :].to_broadcast((128, 1024))),
                        inc=("R", ("csb", l)))

            return dict(gpsimd=f_gp, scalar=f_sc, vector=f_vec, tensor=f_te)

        def ph_wo(e):
            def f_gp(g):
                Bd.wait(g, "C", ("ag2",))
                Bd.op(lambda: g.dma_start(out=ccs_h2, in_=cc_sum[4:8, :]),
                      inc=("X", ("ccs",)))
                Bd.op(lambda: g.dma_start(out=cci_h2, in_=cc_in[4:8, :]),
                      inc=("X", ("cci",)))
                Bd.wait(g, "V", ("ccsub",))
                Bd.op(lambda: g.dma_start(out=cs_part[4:8, :], in_=sub_h2),
                      inc=("X", ("csp",)))
                if debug:
                    Bd.op(lambda: g.dma_start(out=dbg["cs"], in_=cs_own),
                          inc=("X", ("dbgcs",)))
                    Bd.wait(g, "X", ("csp",))
                    Bd.op(lambda: g.dma_start(out=dbg["csp"], in_=cs_part),
                          inc=("X", ("dbgcsp",)))
                for l in range(12, 16):
                    Bd.wait(g, "X", ("csp",))
                    Bd.wait(g, "V", ("att", l - 1))
                    Bd.op(lambda l=l: g.dma_start(
                        out=csb[l % 4],
                        in_=cs_part[l - 8:l - 7, :].to_broadcast((128, 1024))),
                        inc=("R", ("csb", l)))
                for oc in range(NDC):
                    if oc >= 2:
                        Bd.wait(g, "V", ("x1f", oc - 2))
                    Bd.op(lambda oc=oc: g.dma_start(
                        out=xb[oc % 2], in_=xT[oc * 128:(oc + 1) * 128, 0:1024]),
                        inc=("X" if oc % 2 == 0 else "U", ("xre", oc)))

            def f_vec(v):
                Bd.wait(v, "X", ("cci",))
                Bd.op(lambda: v.tensor_sub(sub_h2, ccs_h2, cci_h2),
                      inc=("V", ("ccsub",)))
                for l in range(12, 16):
                    Bd.wait(v, "R", ("csb", l))
                    sl = Vt[:, l * 1024:(l + 1) * 1024]
                    Bd.op(lambda sl=sl, l=l: v.tensor_mul(sl, sl, csb[l % 4]),
                          inc=("V", ("att", l)))
                Bd.wait(v, "A", ("es1",))
                Bd.op(lambda: v.tensor_mul(tmp1, ms, ms))
                Bd.drain(v)
                Bd.op(lambda: v.tensor_sub(es, es, tmp1), inc=("V", ("sub1",)))
                Bd.wait(v, "A", ("sqr1",))
                Bd.op(lambda: v.scalar_tensor_tensor(
                    brep, ms, -1.0, arep, ALU.mult, ALU.mult), inc=("V", ("brep1",)))
                Bd.wait(v, "T", ("wost", NDC - 1))
                for oc in range(NDC):
                    Bd.wait(v, "X" if oc % 2 == 0 else "U", ("xre", oc))
                    if oc >= 4:
                        Bd.wait(v, "A", ("x1c", oc - 4))
                        Bd.wait_maybe(v, YSEM[oc % 4], ("x1d", oc - 4))
                        Bd.wait_maybe(v, XOSEM[oc % 4], ("xo", oc - 4))
                    t1 = gtmp[oc % 2]
                    abf = attn_bf[:, oc * 1024:(oc + 1) * 1024]
                    Bd.op(lambda t1=t1, abf=abf: v.tensor_mul(t1[:, :], abf, arep))
                    Bd.drain(v)
                    Bd.op(lambda t1=t1: v.tensor_add(t1[:, :], t1[:, :], brep))
                    Bd.drain(v)
                    Bd.op(lambda t1=t1, oc=oc: v.tensor_add(
                        x1f[oc % 4], xb[oc % 2], t1[:, :]),
                        inc=("V", ("x1f", oc)))

            def wo_stat_mm(t, oc):
                Bd.wait(t, "A", ("wosq", oc))
                st = (oc == 0)
                sp = (oc == NDC - 1)
                for pc in range(2):
                    Bd.op(lambda oc=oc, pc=pc, st=st, sp=sp: t.matmul(
                        ps[:, 2048 + pc * 512:2048 + (pc + 1) * 512], ones[:, :],
                        attn_bf[:, oc * 1024 + pc * 512:oc * 1024 + (pc + 1) * 512],
                        start=st, stop=sp, skip_group_check=True))
                for pc in range(2):
                    Bd.op(lambda oc=oc, pc=pc, st=st, sp=sp: t.matmul(
                        ps[:, 3072 + pc * 512:3072 + (pc + 1) * 512], ones[:, :],
                        sq[oc % 2][:, pc * 512:(pc + 1) * 512],
                        start=st, stop=sp, skip_group_check=True),
                        inc=("T", ("wost", oc)) if pc == 1 else None)

            def f_te(t):
                for oc in range(NDC):
                    Bd.wait(t, "A", ("wosc", oc))
                    if oc >= 2:
                        Bd.wait(t, "A", ("woev", oc - 2))
                    po = (oc % 2) * 1024
                    for fc in range(16):
                        if oc == 0:
                            Bd.wait(t, "V", ("att", fc))
                        st = (fc == 0)
                        sp = (fc == 15)
                        for pc in range(2):
                            Bd.op(lambda oc=oc, fc=fc, pc=pc, st=st, sp=sp, po=po: t.matmul(
                                ps[:, po + pc * 512:po + (pc + 1) * 512],
                                wbb[oc % 4][:, fc * 128:(fc + 1) * 128],
                                Vt[:, fc * 1024 + pc * 512:fc * 1024 + (pc + 1) * 512],
                                start=st, stop=sp, skip_group_check=True),
                                inc=("T", ("wom", oc)) if (sp and pc == 1) else None)
                    if oc >= 1:
                        wo_stat_mm(t, oc - 1)
                wo_stat_mm(t, NDC - 1)
                Bd.wait(t, "A", ("woev", NDC - 1))
                for oc in range(NDC):
                    Bd.wait(t, "A", ("l2sq", oc))
                    st = (oc == 0)
                    sp = (oc == NDC - 1)
                    for pc in range(2):
                        Bd.op(lambda oc=oc, pc=pc, st=st, sp=sp: t.matmul(
                            ps[:, pc * 512:(pc + 1) * 512], ones[:, :],
                            x1bf[:, oc * 1024 + pc * 512:oc * 1024 + (pc + 1) * 512],
                            start=st, stop=sp, skip_group_check=True))
                    for pc in range(2):
                        Bd.op(lambda oc=oc, pc=pc, st=st, sp=sp: t.matmul(
                            ps[:, 1024 + pc * 512:1024 + (pc + 1) * 512], ones[:, :],
                            sq[oc % 2][:, pc * 512:(pc + 1) * 512],
                            start=st, stop=sp, skip_group_check=True),
                            inc=("T", ("l2st", oc)) if pc == 1 else None)

            def f_sc(s):
                for oc in range(NDC):
                    Bd.wait(s, "T", ("wom", oc))
                    if oc >= 2:
                        Bd.wait(s, "T", ("wost", oc - 2))
                    abf = attn_bf[:, oc * 1024:(oc + 1) * 1024]
                    Bd.op(lambda abf=abf, oc=oc: s.activation(
                        abf, ps[:, (oc % 2) * 1024:(oc % 2) * 1024 + 1024], AF.Copy),
                        inc=("A", ("woev", oc)))
                    Bd.drain(s)
                    Bd.op(lambda abf=abf, oc=oc: s.activation(
                        sq[oc % 2][:, :], abf, AF.Square),
                        inc=("A", ("wosq", oc)))
                    if oc + 2 < NDC:
                        Bd.wait(s, "W" if oc % 2 == 0 else "W2", ("wol", oc + 2))
                        if oc + 2 >= 4:
                            Bd.wait(s, "T", ("wom", oc - 2))
                        Bd.op(lambda oc=oc: s.activation(
                            wbb[(oc + 2) % 4][:, :], wfb[oc % 2], AF.Copy),
                            inc=("A", ("wosc", oc + 2)))
                Bd.wait(s, "T", ("wost", NDC - 1))
                Bd.wait(s, "X", ("csd", 7, 5))
                Bd.op(lambda: s.activation(ms, ps[:, 2048:3072], AF.Copy, scale=RN),
                      inc=("A", ("ms1",)))
                Bd.op(lambda: s.activation(es, ps[:, 3072:4096], AF.Copy, scale=RN),
                      inc=("A", ("es1",)))
                Bd.wait(s, "V", ("sub1",))
                Bd.op(lambda: s.activation(tmp1, es, AF.Ln, bias=eps_t))
                Bd.drain(s)
                Bd.op(lambda: s.activation(arep, tmp1, AF.Exp, scale=-0.5),
                      inc=("A", ("sqr1",)))
                for oc in range(NDC):
                    Bd.wait(s, "V", ("x1f", oc))
                    Bd.op(lambda oc=oc: s.activation(
                        x1bf[:, oc * 1024:(oc + 1) * 1024], x1f[oc % 4], AF.Copy),
                        inc=("A", ("x1c", oc)))
                    if oc < LXD:
                        Bd.op(lambda oc=oc: s.dma_start(
                            out=x1b[oc * 128:(oc + 1) * 128, :], in_=x1f[oc % 4]),
                            inc=(YSEM[oc % 4], ("x1d", oc)))
                    if oc >= RBN or debug:
                        Bd.op(lambda oc=oc: s.dma_start(
                            out=outT[oc * 128:(oc + 1) * 128, :], in_=x1f[oc % 4]),
                            inc=(XOSEM[oc % 4], ("xo", oc)))
                    Bd.drain(s)
                    if oc >= 2:
                        Bd.wait(s, "T", ("l2st", oc - 2))
                    Bd.op(lambda oc=oc: s.activation(
                        sq[oc % 2][:, :], x1bf[:, oc * 1024:(oc + 1) * 1024], AF.Square),
                        inc=("A", ("l2sq", oc)))

            def f_sync(s):
                for oc in range(2, NDC):
                    Bd.wait(s, "A", ("wosc", oc - 2))
                    Bd.op(lambda oc=oc: s.dma_start(out=wfb[oc % 2], in_=WO[oc]),
                          inc=("W" if oc % 2 == 0 else "W2", ("wol", oc)))

            return dict(sync=f_sync, gpsimd=f_gp, scalar=f_sc, vector=f_vec, tensor=f_te)

        def ph_ln2(e):
            def f_sc(s):
                Bd.wait(s, "T", ("l2st", NDC - 1))
                Bd.op(lambda: s.activation(ms, ps[:, 0:1024], AF.Copy, scale=RN),
                      inc=("A", ("ms2",)))
                Bd.op(lambda: s.activation(es, ps[:, 1024:2048], AF.Copy, scale=RN),
                      inc=("A", ("es2",)))
                Bd.wait(s, "V", ("sub2",))
                Bd.op(lambda: s.activation(tmp1, es, AF.Ln, bias=eps_t))
                Bd.drain(s)
                Bd.op(lambda: s.activation(arep, tmp1, AF.Exp, scale=-0.5),
                      inc=("A", ("sqr2",)))
                Bd.wait(s, "S", ("dsl",))
                Bd.wait(s, "W", ("gl", 0))
                Bd.wait(s, "T", ("wom", NDC - 1))
                Bd.op(lambda: s.activation(gwb[0][:, :], gwf[0], AF.Copy),
                      inc=("A", ("gcs", 0)))
                Bd.wait(s, "W3", ("ul", 0))
                Bd.op(lambda: s.activation(uwb[0][:, :], uwf[0], AF.Copy),
                      inc=("A", ("ucs", 0)))
                Bd.wait(s, "W2", ("gl", 1))
                Bd.op(lambda: s.activation(gwb[1][:, :], gwf[1], AF.Copy),
                      inc=("A", ("gcs", 1)))
                Bd.wait(s, "W4", ("ul", 1))
                Bd.op(lambda: s.activation(uwb[1][:, :], uwf[1], AF.Copy),
                      inc=("A", ("ucs", 1)))

            def f_vec(v):
                Bd.wait(v, "A", ("es2",))
                Bd.op(lambda: v.tensor_mul(tmp1, ms, ms))
                Bd.drain(v)
                Bd.op(lambda: v.tensor_sub(es, es, tmp1), inc=("V", ("sub2",)))
                Bd.wait(v, "A", ("sqr2",))
                Bd.op(lambda: v.scalar_tensor_tensor(
                    brep, ms, -1.0, arep, ALU.mult, ALU.mult), inc=("V", ("brep2",)))
                Bd.wait(v, "T", ("l2st", NDC - 1))
                for oc in range(NDC):
                    sl = x1bf[:, oc * 1024:(oc + 1) * 1024]
                    Bd.op(lambda sl=sl: v.tensor_mul(sl, sl, arep))
                    Bd.drain(v)
                    Bd.op(lambda sl=sl: v.tensor_add(sl, sl, brep),
                          inc=("V", ("ln2", oc)))

            return dict(scalar=f_sc, vector=f_vec)

        def ph_gu(e):
            def f_sync(s):
                for ic in range(NIC):
                    if ic == 0:
                        Bd.wait(s, "A", ("wosc", NDC - 2))
                    if ic == 1:
                        Bd.wait(s, "A", ("wosc", NDC - 1))
                    if ic == 2:
                        Bd.wait(s, "A", ("gcs", 0))
                        Bd.wait(s, "A", ("ucs", 0))
                    if ic == 3:
                        Bd.wait(s, "A", ("gcs", 1))
                        Bd.wait(s, "A", ("ucs", 1))
                    if ic >= 4:
                        Bd.wait(s, "V", ("gcast", ic - 2))
                        Bd.wait(s, "V", ("ucast", ic - 2))
                    if ic == 0:
                        Bd.wait(s, "T", ("wom", NDC - 1))
                    Bd.op(lambda ic=ic: s.dma_start(out=gwf[ic % 2], in_=GW[ic]),
                          inc=("W" if ic % 2 == 0 else "W2", ("gl", ic)))
                    Bd.op(lambda ic=ic: s.dma_start(out=uwf[ic % 2], in_=UW[ic]),
                          inc=("W3" if ic % 2 == 0 else "W4", ("ul", ic)))
                Bd.op(lambda: s.dma_start(out=dwf_pre, in_=DW[0]),
                      inc=("W5", ("dl", 0)))

            def f_vec(v):
                for ic in range(NIC):
                    if ic >= 2:
                        Bd.wait(v, "W" if ic % 2 == 0 else "W2", ("gl", ic))
                        Bd.wait(v, "T", ("gm", ic - 2))
                        Bd.op(lambda ic=ic: v.tensor_copy(gwb[ic % 2][:, :], gwf[ic % 2]),
                              inc=("V", ("gcast", ic)))
                        Bd.wait(v, "W3" if ic % 2 == 0 else "W4", ("ul", ic))
                        Bd.wait(v, "T", ("um", ic - 2))
                        Bd.op(lambda ic=ic: v.tensor_copy(uwb[ic % 2][:, :], uwf[ic % 2]),
                              inc=("V", ("ucast", ic)))
                    if ic >= 1:
                        j = ic - 1
                        if j == 0:
                            Bd.wait(v, "T", ("gm", 0))
                        if j == 20:
                            for i in range(4):
                                if LX1D[i]:
                                    Bd.wait(v, YSEM[i], ("x1d", LX1D[i][-1]))
                                if LXO[i]:
                                    Bd.wait(v, XOSEM[i], ("xo", LXO[i][-1]))
                        Bd.wait(v, "A", ("gev", j))
                        Bd.wait(v, "A", ("uev", j))
                        Bd.op(lambda j=j: v.tensor_mul(
                            ht[:, j * 1024:(j + 1) * 1024], gtmp[j % 2][:, :],
                            utmp[j % 2][:, :]),
                            inc=("V", ("h", j)))
                j = NIC - 1
                Bd.wait(v, "A", ("gev", j))
                Bd.wait(v, "A", ("uev", j))
                Bd.op(lambda j=j: v.tensor_mul(
                    ht[:, j * 1024:(j + 1) * 1024], gtmp[j % 2][:, :], utmp[j % 2][:, :]),
                    inc=("V", ("h", j)))

            def f_te(t):
                for ic in range(NIC):
                    if ic < 2:
                        Bd.wait(t, "A", ("gcs", ic))
                    else:
                        Bd.wait(t, "V", ("gcast", ic))
                        Bd.wait(t, "A", ("gev", ic - 2))
                    if ic == 0:
                        Bd.wait(t, "A", ("es2",))
                        Bd.wait(t, "A", ("es1",))
                    po = (ic % 2) * 2048
                    for dc in range(NDC):
                        if ic == 0:
                            Bd.wait(t, "V", ("ln2", dc))
                        st = (dc == 0)
                        sp = (dc == NDC - 1)
                        for pc in range(2):
                            Bd.op(lambda ic=ic, dc=dc, pc=pc, st=st, sp=sp, po=po: t.matmul(
                                ps[:, po + pc * 512:po + (pc + 1) * 512],
                                gwb[ic % 2][:, dc * 128:(dc + 1) * 128],
                                x1bf[:, dc * 1024 + pc * 512:dc * 1024 + (pc + 1) * 512],
                                start=st, stop=sp, skip_group_check=True),
                                inc=("T", ("gm", ic)) if (sp and pc == 1) else None)
                    if ic < 2:
                        Bd.wait(t, "A", ("ucs", ic))
                    else:
                        Bd.wait(t, "V", ("ucast", ic))
                        Bd.wait(t, "A", ("uev", ic - 2))
                    for dc in range(NDC):
                        st = (dc == 0)
                        sp = (dc == NDC - 1)
                        for pc in range(2):
                            Bd.op(lambda ic=ic, dc=dc, pc=pc, st=st, sp=sp, po=po: t.matmul(
                                ps[:, po + 1024 + pc * 512:po + 1024 + (pc + 1) * 512],
                                uwb[ic % 2][:, dc * 128:(dc + 1) * 128],
                                x1bf[:, dc * 1024 + pc * 512:dc * 1024 + (pc + 1) * 512],
                                start=st, stop=sp, skip_group_check=True),
                                inc=("T", ("um", ic)) if (sp and pc == 1) else None)

            def f_sc(s):
                for ic in range(NIC):
                    Bd.wait(s, "T", ("gm", ic))
                    if ic >= 2:
                        Bd.wait(s, "V", ("h", ic - 2))
                    po = (ic % 2) * 2048
                    Bd.op(lambda ic=ic, po=po: s.activation(
                        gtmp[ic % 2][:, :], ps[:, po:po + 1024], AF.Silu,
                        scale=gs_t[:, ic:ic + 1]),
                        inc=("A", ("gev", ic)))
                    Bd.wait(s, "T", ("um", ic))
                    Bd.op(lambda ic=ic, po=po: s.activation(
                        utmp[ic % 2][:, :], ps[:, po + 1024:po + 2048], AF.Copy,
                        scale=us_t[:, ic:ic + 1]),
                        inc=("A", ("uev", ic)))

            return dict(sync=f_sync, scalar=f_sc, vector=f_vec, tensor=f_te)

        def ph_dn(e):
            def f_sync(s):
                for oc in range(1, NDC):
                    if oc >= 2:
                        Bd.wait(s, "V", ("dcast", oc - 1))
                    else:
                        Bd.wait(s, "V", ("gcast", NIC - 1))
                        Bd.wait(s, "T", ("gm", NIC - 1))
                    Bd.op(lambda oc=oc: s.dma_start(out=dwf, in_=DW[oc]),
                          inc=("W5" if oc % 2 == 0 else "W6", ("dl", oc)))
                Bd.wait(s, "Y", ("x1d", LX1D[0][-1]))
                Bd.wait(s, "Y2", ("x1d", LX1D[1][-1]))
                Bd.wait(s, "W", ("gl", NIC - 2))
                Bd.wait(s, "W2", ("gl", NIC - 1))
                for j in range(RBN):
                    if j < 2:
                        Bd.wait(s, "V", ("dcast", 0))
                    else:
                        if j == 4:
                            Bd.wait(s, "W5", ("dl", NDC - 2))
                        if j == 5:
                            Bd.wait(s, "W6", ("dl", NDC - 1))
                        Bd.wait(s, "V", ("dcast", NDC - 1))
                    Bd.op(lambda j=j: s.dma_start(
                        out=xrb[j], in_=x1b[j * 128:(j + 1) * 128, :]),
                        inc=(RB_SEM[j], ("xrb", j)))

            def f_vec(v):
                for oc in range(NDC):
                    Bd.wait(v, "W5" if oc % 2 == 0 else "W6", ("dl", oc))
                    if oc >= 2:
                        Bd.wait(v, "T", ("dm", oc - 2))
                    elif oc == 0:
                        Bd.wait(v, "T", ("gm", NIC - 1))
                        Bd.wait(v, "T", ("um", NIC - 1))
                    src = dwf_pre if oc == 0 else dwf
                    Bd.op(lambda oc=oc, src=src: v.tensor_copy(dwb[oc % 2][:, :], src),
                          inc=("V", ("dcast", oc)))

            def dn_stat_mm(t, oc):
                Bd.wait(t, "A", ("dsq", oc))
                st = (oc == 0)
                sp = (oc == NDC - 1)
                for pc in range(2):
                    Bd.op(lambda oc=oc, pc=pc, st=st, sp=sp: t.matmul(
                        ps[:, 2048 + pc * 512:2048 + (pc + 1) * 512], ones[:, :],
                        ffn_bf[:, oc * 1024 + pc * 512:oc * 1024 + (pc + 1) * 512],
                        start=st, stop=sp, skip_group_check=True))
                for pc in range(2):
                    Bd.op(lambda oc=oc, pc=pc, st=st, sp=sp: t.matmul(
                        ps[:, 3072 + pc * 512:3072 + (pc + 1) * 512], ones[:, :],
                        sq[oc % 2][:, pc * 512:(pc + 1) * 512],
                        start=st, stop=sp, skip_group_check=True),
                        inc=("T", ("dst", oc)) if pc == 1 else None)

            def f_te(t):
                for oc in range(NDC):
                    Bd.wait(t, "V", ("dcast", oc))
                    if oc == 0:
                        Bd.wait(t, "V", ("h", NIC - 1))
                    if oc >= 2:
                        Bd.wait(t, "A", ("dev", oc - 2))
                    po = (oc % 2) * 1024
                    for ic in range(NIC):
                        st = (ic == 0)
                        sp = (ic == NIC - 1)
                        for pc in range(2):
                            Bd.op(lambda oc=oc, ic=ic, pc=pc, st=st, sp=sp, po=po: t.matmul(
                                ps[:, po + pc * 512:po + (pc + 1) * 512],
                                dwb[oc % 2][:, ic * 128:(ic + 1) * 128],
                                ht[:, ic * 1024 + pc * 512:ic * 1024 + (pc + 1) * 512],
                                start=st, stop=sp, skip_group_check=True),
                                inc=("T", ("dm", oc)) if (sp and pc == 1) else None)
                    if oc >= 1:
                        dn_stat_mm(t, oc - 1)
                dn_stat_mm(t, NDC - 1)

            def f_sc(s):
                Bd.wait(s, "S", ("dsl",))
                for oc in range(NDC):
                    Bd.wait(s, "T", ("dm", oc))
                    if oc >= 2:
                        Bd.wait(s, "T", ("dst", oc - 2))
                    fbf = ffn_bf[:, oc * 1024:(oc + 1) * 1024]
                    Bd.op(lambda fbf=fbf, oc=oc: s.activation(
                        fbf, ps[:, (oc % 2) * 1024:(oc % 2) * 1024 + 1024], AF.Copy,
                        scale=ds_t[:, oc:oc + 1]),
                        inc=("A", ("dev", oc)))
                    Bd.drain(s)
                    Bd.op(lambda fbf=fbf, oc=oc: s.activation(
                        sq[oc % 2][:, :], fbf, AF.Square),
                        inc=("A", ("dsq", oc)))

            return dict(sync=f_sync, scalar=f_sc, vector=f_vec, tensor=f_te)

        def ph_fi(e):
            def f_sc(s):
                Bd.wait(s, "T", ("dst", NDC - 1))
                Bd.op(lambda: s.activation(ms, ps[:, 2048:3072], AF.Copy, scale=RN),
                      inc=("A", ("ms3",)))
                Bd.op(lambda: s.activation(es, ps[:, 3072:4096], AF.Copy, scale=RN),
                      inc=("A", ("es3",)))
                Bd.wait(s, "V", ("sub3",))
                Bd.op(lambda: s.activation(tmp1, es, AF.Ln, bias=eps_t))
                Bd.drain(s)
                Bd.op(lambda: s.activation(arep, tmp1, AF.Exp, scale=-0.5),
                      inc=("A", ("sqr3",)))


            def f_vec(v):
                Bd.wait(v, "A", ("es3",))
                Bd.op(lambda: v.tensor_mul(tmp1, ms, ms))
                Bd.drain(v)
                Bd.op(lambda: v.tensor_sub(es, es, tmp1), inc=("V", ("sub3",)))
                Bd.wait(v, "A", ("sqr3",))
                Bd.op(lambda: v.scalar_tensor_tensor(
                    brep, ms, -1.0, arep, ALU.mult, ALU.mult), inc=("V", ("brep3",)))
                Bd.wait(v, "T", ("dst", NDC - 1))
                for i, oc in enumerate(FI_SEQ):
                    if i >= 4:
                        poc = FI_SEQ[i - 4]
                        if poc < RBN:
                            Bd.wait(v, RB_SEM[poc], ("wout", poc))
                        else:
                            Bd.wait(v, ["O", "O2", "O3", "O4"][FI_SLOT[poc]],
                                    ("out", poc))
                    t1 = sq[i % 2]
                    fbf = ffn_bf[:, oc * 1024:(oc + 1) * 1024]
                    Bd.op(lambda t1=t1, fbf=fbf: v.tensor_mul(t1[:, :], fbf, arep))
                    Bd.drain(v)
                    Bd.op(lambda t1=t1, oc=oc: v.tensor_add(
                        ostage[FI_SLOT[oc]], t1[:, :], brep),
                        inc=("V", ("ost", oc)))
                    if oc < RBN:
                        Bd.wait(v, RB_SEM[oc], ("xrb", oc))
                        Bd.drain(v)
                        Bd.op(lambda oc=oc: v.tensor_add(
                            ostage[FI_SLOT[oc]], ostage[FI_SLOT[oc]], xrb[oc]),
                            inc=("V", ("fin", oc)))

            def f_sync(s):
                for oc in range(RBN):
                    Bd.wait(s, "V", ("fin", oc))
                    Bd.op(lambda oc=oc: s.dma_start(
                        out=outT[oc * 128:(oc + 1) * 128, :],
                        in_=ostage[FI_SLOT[oc]]),
                        inc=(RB_SEM[oc], ("wout", oc)))

            def f_gp(g):
                for i in range(4):
                    if LXO[i]:
                        Bd.wait(g, XOSEM[i], ("xo", LXO[i][-1]))
                for oc in range(RBN, NDC):
                    Bd.wait(g, "V", ("ost", oc))
                    Bd.op(lambda oc=oc: g.dma_start(
                        out=outT[oc * 128:(oc + 1) * 128, :],
                        in_=ostage[FI_SLOT[oc]],
                        accum_op=ALU.add),
                        inc=(["O", "O2", "O3", "O4"][FI_SLOT[oc]], ("out", oc)))
                if debug:
                    for i in range(4):
                        if LX1D[i]:
                            Bd.wait(g, YSEM[i], ("x1d", LX1D[i][-1]))
                    Bd.op(lambda: g.dma_start(out=dbg["x1"], in_=x1b),
                          inc=("X", ("dbgx1",)))
                lastO = {}
                for oc in range(RBN, NDC):
                    lastO[FI_SLOT[oc]] = oc
                for slot, oc in lastO.items():
                    Bd.wait(g, ["O", "O2", "O3", "O4"][slot], ("out", oc))
                for oc in range(RBN):
                    Bd.wait(g, RB_SEM[oc], ("wout", oc))

            return dict(scalar=f_sc, vector=f_vec, sync=f_sync, gpsimd=f_gp)

        def ph_dbg_a(e):
            def f_gp(g):
                Bd.op(lambda: g.dma_start(out=dbg["q"], in_=Qt), inc=("X", ("dbgq",)))
                Bd.op(lambda: g.dma_start(out=dbg["v"], in_=Vt), inc=("X", ("dbgv",)))
                Bd.wait(g, "X", ("dbgv",))
            return dict(gpsimd=f_gp)

        def ph_bat(e):
            parts = [ph_lnqkv(e, 1), ph_att(e), ph_wo(e), ph_ln2(e), ph_gu(e)]
            names = ("sync", "gpsimd", "scalar", "vector", "tensor")
            out = {}
            for nme in names:
                fs = [p[nme] for p in parts if nme in p]
                if fs:
                    out[nme] = (lambda fs: lambda eng: [f(eng) for f in fs])(fs)
            return out

        def ph_a(e):
            parts = [ph_setup(e), ph_lnqkv(e, 0),
                     ph_lnqkv(e, 1), ph_att(e), ph_wo(e), ph_ln2(e), ph_gu(e)]
            names = ("sync", "gpsimd", "scalar", "vector", "tensor")
            out = {}
            for nme in names:
                fs = [p[nme] for p in parts if nme in p]
                if fs:
                    out[nme] = (lambda fs: lambda eng: [f(eng) for f in fs])(fs)
            return out

        phases = [ph_a]
        if debug:
            phases.append(ph_dbg_a)
        phases += [ph_dn, ph_fi]
        def ph_wog(e):
            parts = [ph_wo(e), ph_ln2(e), ph_gu(e)]
            names = ("sync", "gpsimd", "scalar", "vector", "tensor")
            out = {}
            for nme in names:
                fs = [p[nme] for p in parts if nme in p]
                if fs:
                    out[nme] = (lambda fs: lambda eng: [f(eng) for f in fs])(fs)
            return out


        Bd.mode = "count"
        for ph in phases:
            bodies = ph(None)
            for name in ("sync", "gpsimd", "scalar", "vector", "tensor"):
                if name in bodies:
                    bodies[name](None)

        Bd.mode = "emit"
        for ph in phases:
            bodies = ph(None)
            with nc.Block() as blk:
                if "sync" in bodies:
                    blk.sync(bodies["sync"])
                if "gpsimd" in bodies:
                    blk.gpsimd(bodies["gpsimd"])
                if "scalar" in bodies:
                    blk.scalar(bodies["scalar"])
                if "vector" in bodies:
                    blk.vector(bodies["vector"])
                if "tensor" in bodies:
                    blk.tensor(bodies["tensor"])

    return nc


def _blocks(W):
    Kin, Mout = W.shape
    nK, nM = Kin // 128, Mout // 128
    return np.ascontiguousarray(
        W.reshape(nK, 128, nM, 128).transpose(2, 1, 0, 3).reshape(nM, 128, nK * 128))


_CACHE = {}


def kernel(L0_attn_x, L0_attn_w_qkv, L0_attn_w_o, L0_ffn_gate_w, L0_ffn_up_w,
           L0_ffn_down_w, L0_ffn_gate_s, L0_ffn_up_s, L0_ffn_down_s):
    x = np.asarray(L0_attn_x, dtype=np.float32)
    w_qkv = np.asarray(L0_attn_w_qkv, dtype=np.float32)
    w_o = np.asarray(L0_attn_w_o, dtype=np.float32)
    gate_w = np.asarray(L0_ffn_gate_w, dtype=np.float32)
    up_w = np.asarray(L0_ffn_up_w, dtype=np.float32)
    down_w = np.asarray(L0_ffn_down_w, dtype=np.float32)
    gate_s = np.asarray(L0_ffn_gate_s, dtype=np.float32)
    up_s = np.asarray(L0_ffn_up_s, dtype=np.float32)
    down_s = np.asarray(L0_ffn_down_s, dtype=np.float32)

    if "nc" not in _CACHE:
        _CACHE["nc"] = build()
    nc = _CACHE["nc"]

    gw_b = _blocks(np.ascontiguousarray(gate_w.T))
    uw_b = _blocks(np.ascontiguousarray(up_w.T))
    dw_b = _blocks(np.ascontiguousarray(down_w.T))
    gs_r = np.ascontiguousarray(gate_s.reshape(NIC, 128).T)
    us_r = np.ascontiguousarray(up_s.reshape(NIC, 128).T)
    ds_r = np.ascontiguousarray(down_s.reshape(NDC, 128).T)

    wq_h = [_blocks(np.ascontiguousarray(w_qkv[:, hh * 1024:(hh + 1) * 1024]))
            for hh in (0, 1)]
    wk_h = [_blocks(np.ascontiguousarray(w_qkv[:, 2048 + hh * 1024:2048 + (hh + 1) * 1024]))
            for hh in (0, 1)]
    wv_full = w_qkv[:, 4096:6144]
    wv_h = []
    wo_h = []
    for hh in (0, 1):
        colperm = (np.concatenate([wv_full[:, 1024:], wv_full[:, :1024]], axis=1)
                   if hh else wv_full)
        wv_h.append(_blocks(np.ascontiguousarray(colperm)))
        rowperm = (np.concatenate([w_o[1024:, :], w_o[:1024, :]], axis=0)
                   if hh else w_o)
        wo_h.append(_blocks(np.ascontiguousarray(rowperm)))

    in_maps = []
    xT_c = {}
    for c in range(8):
        b, hh = c >> 1, c & 1
        key = (b, hh)
        if key not in xT_c:
            xt = x[b].T
            if hh:
                xt = np.concatenate([xt[:, 1024:], xt[:, :1024]], axis=1)
            xT_c[key] = np.ascontiguousarray(xt)
        in_maps.append(dict(
            xT=xT_c[key], WQ=wq_h[hh], WK=wk_h[hh], WV=wv_h[hh], WOw=wo_h[hh],
            GW=gw_b, UW=uw_b, DW=dw_b, GSs=gs_r, USs=us_r, DSs=ds_r))

    res = run_bass_kernel_spmd(nc, in_maps, core_ids=list(range(8)))

    out = np.empty((B, S, D), dtype=np.float32)
    for c in range(8):
        b, hh = c >> 1, c & 1
        out[b, hh * 1024:(hh + 1) * 1024, :] = res.results[c]["outT"].T
    return out


# version 43 (final)
# speedup vs baseline: 1.0058x; 1.0058x over previous
import contextlib
import numpy as np
import ml_dtypes

import concourse.bass as bass
import concourse.mybir as mybir
from concourse.bass_utils import run_bass_kernel_spmd

F32 = mybir.dt.float32
BF16 = mybir.dt.bfloat16
AF = mybir.ActivationFunctionType
ALU = mybir.AluOpType

B, S, D = 4, 2048, 2048
H, DH = 16, 128
I = 5632
NDC = D // 128
NIC = I // 128
T = 1024
ISQ = float(1.0 / np.sqrt(DH))
EPS = 1e-5
RN = float(1.0 / D)

O_ONES = 0
O_GS = 64
O_US = 108
O_DS = 152
O_Z = 168
O_XB = 192
O_SQ = 2240
O_MS = 3264
O_ES = 4288
O_TMP = 5312
O_AREP = 6336
O_BREP = 6848
O_WZ = 7360
O_V = 19648
O_Q = 27840
O_K = 36032
O_H = 19648
O_XH = 44224
POOL_N = 52416

SEM_STEP = dict(W=16, W2=16, W3=16, W4=16, W5=16, W6=16, S=16, X=16, U=16, R=16, Y=16, Y2=16, Y3=16, Y4=16, O=16, O2=16, O3=16, O4=16, XO=16, XO2=16, XO3=16, XO4=16, T=1, A=1, V=1, C=1)


class Bld:

    def __init__(self):
        self.mode = "count"
        self.cnt = {k: 0 for k in SEM_STEP}
        self.ev = {}
        self.sems = None

    def op(self, fn, inc=None, waits=()):
        if self.mode == "count":
            if inc is not None:
                sem = inc[0]
                self.cnt[sem] += SEM_STEP[sem]
                key = (sem,) + tuple(inc[1])
                assert key not in self.ev, f"dup event {key}"
                self.ev[key] = self.cnt[sem]
            return None
        ins = fn()
        if inc is not None:
            ins.then_inc(self.sems[inc[0]], SEM_STEP[inc[0]])
        return ins

    def wait(self, eng, sem, tag):
        if self.mode == "count":
            return
        key = (sem,) + tuple(tag)
        eng.wait_ge(self.sems[sem], self.ev[key])

    def wait_maybe(self, eng, sem, tag):
        if self.mode == "count":
            return
        key = (sem,) + tuple(tag)
        if key in self.ev:
            eng.wait_ge(self.sems[sem], self.ev[key])

    def drain(self, eng):
        if self.mode == "emit":
            eng.drain()


def build(debug=False):
    nc = bass.Bass("TRN2", target_bir_lowering=False)

    xT = nc.dram_tensor("xT", [D, S], F32, kind="ExternalInput").ap()
    WQ = nc.dram_tensor("WQ", [8, 128, NDC * 128], F32, kind="ExternalInput").ap()
    WK = nc.dram_tensor("WK", [8, 128, NDC * 128], F32, kind="ExternalInput").ap()
    WV = nc.dram_tensor("WV", [16, 128, NDC * 128], F32, kind="ExternalInput").ap()
    WO = nc.dram_tensor("WOw", [16, 128, NDC * 128], F32, kind="ExternalInput").ap()
    GW = nc.dram_tensor("GW", [NIC, 128, NDC * 128], F32, kind="ExternalInput").ap()
    UW = nc.dram_tensor("UW", [NIC, 128, NDC * 128], F32, kind="ExternalInput").ap()
    DW = nc.dram_tensor("DW", [NDC, 128, NIC * 128], F32, kind="ExternalInput").ap()
    GSs = nc.dram_tensor("GSs", [128, NIC], F32, kind="ExternalInput").ap()
    USs = nc.dram_tensor("USs", [128, NIC], F32, kind="ExternalInput").ap()
    DSs = nc.dram_tensor("DSs", [128, NDC], F32, kind="ExternalInput").ap()

    outT = nc.dram_tensor("outT", [D, T], F32, kind="ExternalOutput").ap()

    x1b = nc.dram_tensor("x1b", [D, T], F32).ap()
    cs_own = nc.dram_tensor("cs_own", [8, S], F32).ap()
    cs_part = nc.dram_tensor("cs_part", [8, T], F32).ap()
    cc_in = nc.dram_tensor("cc_in", [8, T], F32).ap()
    cc_sum = nc.dram_tensor("cc_sum", [8, T], F32).ap()

    dbg = {}
    if debug:
        dbg["q"] = nc.dram_tensor("dbg_q", [128, 8 * 2048], BF16, kind="ExternalOutput").ap()
        dbg["v"] = nc.dram_tensor("dbg_v", [128, 16 * 1024], BF16, kind="ExternalOutput").ap()
        dbg["cs"] = nc.dram_tensor("dbg_cs", [8, S], F32, kind="ExternalOutput").ap()
        dbg["csp"] = nc.dram_tensor("dbg_csp", [8, T], F32, kind="ExternalOutput").ap()
        dbg["x1"] = nc.dram_tensor("dbg_x1", [D, T], F32, kind="ExternalOutput").ap()

    Bd = Bld()
    RBN = 6
    RB_SEM = ["Y", "Y2", "W", "W2", "W5", "W6"]
    LXD = NDC if debug else RBN
    YSEM = ["Y", "Y2", "Y3", "Y4"]
    XOSEM = ["XO", "XO2", "XO3", "XO4"]
    LX1D = {i: [oc for oc in range(LXD) if oc % 4 == i] for i in range(4)}
    LXO = {i: [oc for oc in range((0 if debug else RBN), NDC) if oc % 4 == i]
           for i in range(4)}
    FI_SEQ = []
    for _k in range(RBN):
        FI_SEQ += [RBN + _k, _k]
    FI_SEQ += list(range(2 * RBN, NDC))
    FI_SLOT = {oc: i % 4 for i, oc in enumerate(FI_SEQ)}

    with contextlib.ExitStack() as ctx:
        pool = ctx.enter_context(nc.sbuf_tensor("pool", [128, POOL_N], F32))

        def f32(off, n):
            return pool[:, off:off + n]

        def bf(off, nf32):
            return pool[:, off:off + nf32].bitcast(BF16)

        ones = bf(O_ONES, 64)
        gs_t = f32(O_GS, NIC)
        us_t = f32(O_US, NIC)
        ds_t = f32(O_DS, NDC)
        z1 = [f32(O_Z + i, 1) for i in range(8)]
        zf = f32(O_Z + 8, 1)
        rt = f32(O_Z + 9, 1)
        rbv = bf(O_Z + 10, 4)
        rb = [rbv[:, 2 * i:2 * i + 1] for i in range(4)]
        eps_t = f32(O_Z + 14, 1)
        xb = [f32(O_XB, 1024), f32(O_XB + 1024, 1024)]
        sq = [bf(O_SQ, 512), bf(O_SQ + 512, 512)]
        xbB3 = [xb[0], xb[1], f32(O_SQ, 1024)]
        XSEM3 = ["X", "U", "R"]
        ms = f32(O_MS, 1024)
        es = f32(O_ES, 1024)
        tmp1 = f32(O_TMP, 1024)
        arep = bf(O_AREP, 512)
        brep = bf(O_BREP, 512)
        wfb = [f32(O_WZ, 2048), f32(O_WZ + 2048, 2048)]
        wbb = [bf(O_WZ + 4096 + 1024 * i, 1024) for i in range(4)]
        gwf = wfb
        gwb = wbb
        uwf = [f32(O_WZ + 6144, 2048), f32(O_WZ + 8192, 2048)]
        uwb = [bf(O_WZ + 10240, 1024), bf(O_WZ + 11264, 1024)]
        dwf = f32(O_WZ, 5632)
        dwf_pre = f32(O_XB, 5632)
        dwb = [bf(O_WZ + 5632, 2816), bf(O_WZ + 8448, 2816)]
        Vt = bf(O_V, 8192)
        Qt = bf(O_Q, 8192)
        Kt = bf(O_K, 8192)
        attn_bf = bf(O_Q, 8192)
        csb_at = [f32(O_TMP, 1024), f32(O_TMP + 1024, 1024)]
        csb = [f32(O_K + 1024 * i, 1024) for i in range(4)]
        cci_sb = pool[0:8, O_K:O_K + 1024]
        ccs_sb = pool[0:8, O_K + 1024:O_K + 2048]
        sub_sb = pool[0:8, O_K + 2048:O_K + 3072]
        cci_h2 = pool[32:36, O_K:O_K + 1024]
        ccs_h2 = pool[32:36, O_K + 1024:O_K + 2048]
        sub_h2 = pool[32:36, O_K + 2048:O_K + 3072]
        x1f = [f32(O_K + 4096, 1024), f32(O_K + 5120, 1024),
               f32(O_K, 1024), f32(O_K + 1024, 1024)]
        ostage = [f32(O_K + 2048 + 1024 * i, 1024) for i in range(4)]
        xrb = [f32(O_XB, 1024), f32(O_XB + 1024, 1024)] + \
              [f32(O_WZ + 1024 * i, 1024) for i in range(4)]
        gtmp = [bf(O_K + 6144, 512), bf(O_K + 6656, 512)]
        utmp = [bf(O_K + 7168, 512), bf(O_K + 7680, 512)]
        xh = bf(O_XH, 8192)
        Et = [bf(O_WZ + 8192 + 1024 * i, 1024) for i in range(4)]
        xBb = [bf(O_WZ + 8192, 512), bf(O_WZ + 8704, 512)]
        sqB = [bf(O_WZ + 9216, 512), bf(O_WZ + 9728, 512)]
        cstage = pool[0:33, O_MS:O_MS + 2048]
        x1bf = bf(O_XH, 8192)
        ffn_bf = x1bf
        ht = bf(O_H, 22528)

        ps = ctx.enter_context(nc.psum_tensor("ps", [128, 4096], F32))

        sems = {}
        for name in SEM_STEP:
            sems[name] = ctx.enter_context(nc.semaphore("s" + name))
        Bd.sems = sems


        def ph_setup(e):
            def f_sync(s):
                Bd.op(lambda: s.dma_start(out=gs_t, in_=GSs), inc=("S", ("gsl",)))
                Bd.op(lambda: s.dma_start(out=us_t, in_=USs), inc=("S", ("usl",)))
                Bd.op(lambda: s.dma_start(out=ds_t, in_=DSs), inc=("S", ("dsl",)))
            def f_vec(v):
                Bd.op(lambda: v.memset(eps_t, EPS))
                Bd.drain(v)
                Bd.op(lambda: v.memset(ones[:, :], 1.0), inc=("V", ("ones",)))
            return dict(sync=f_sync, vector=f_vec)

        def ph_lnqkv(e, pa):
            coloff = pa * 1024
            fcs = []
            if pa == 0:
                for i in range(8):
                    fcs.append(("q", WQ, i))
                for i in range(8):
                    fcs.append(("k", WK, i))
                for i in range(16):
                    fcs.append(("v", WV, i))
            else:
                for i in range(8):
                    fcs.append(("q", WQ, i))
                    fcs.append(("k", WK, i))
            nfc = len(fcs)

            def f_gp(g):
                if pa == 1:
                    for dc in range(2, NDC):
                        if dc == 2:
                            Bd.wait(g, "T", ("stat", 0, NDC - 1))
                        if dc >= 3:
                            Bd.wait(g, "V", ("lncast", 1, dc - 3))
                        Bd.op(lambda dc=dc: g.dma_start(
                            out=xbB3[dc % 3],
                            in_=xT[dc * 128:(dc + 1) * 128, 1024:2048]),
                            inc=(XSEM3[dc % 3], ("xld", 1, dc)))
                for dc in range(NDC if pa == 1 else 0, NDC):
                    if dc >= 2:
                        Bd.wait(g, "V", ("lncast", pa, dc - 2))
                    Bd.op(lambda dc=dc: g.dma_start(
                        out=xb[dc % 2],
                        in_=xT[dc * 128:(dc + 1) * 128, coloff:coloff + 1024]),
                        inc=("X" if dc % 2 == 0 else "U", ("xld", pa, dc)))
                if pa == 0:
                    for dc in range(NDC):
                        Bd.wait(g, "V", ("lncast", 0, 14 + dc) if dc < 2
                                else ("xcB", dc - 2))
                        Bd.op(lambda dc=dc: g.dma_start(
                            out=xb[dc % 2],
                            in_=xT[dc * 128:(dc + 1) * 128, 1024:2048]),
                            inc=("X" if dc % 2 == 0 else "U", ("xldB", dc)))
                    Bd.wait(g, "V", ("xcB", NDC - 2))
                    Bd.op(lambda: g.dma_start(
                        out=xb[0], in_=xT[0:128, 1024:2048]),
                        inc=("X", ("xld", 1, 0)))
                    Bd.wait(g, "V", ("xcB", NDC - 1))
                    Bd.op(lambda: g.dma_start(
                        out=xb[1], in_=xT[128:256, 1024:2048]),
                        inc=("U", ("xld", 1, 1)))

            def f_vec(v):
                if pa == 1:
                    Bd.wait(v, "T", ("qkv", 0, 31))
                    def wc(fc):
                        Bd.wait(v, "W" if fc % 2 == 0 else "W2", ("wld", 1, fc))
                        Bd.op(lambda fc=fc: v.tensor_copy(wbb[fc % 4][:, :], wfb[fc % 2]),
                              inc=("V", ("wcast", 1, fc)))
                    wc(0)
                    wc(1)
                    for dc in range(NDC):
                        Bd.wait(v, XSEM3[dc % 3], ("xld", 1, dc))
                        sl = xh[:, dc * 1024:(dc + 1) * 1024]
                        Bd.op(lambda sl=sl, dc=dc: v.tensor_mul(sl, xbB3[dc % 3], arep),
                              inc=("V", ("lncast", 1, dc)))
                        Bd.drain(v)
                        Bd.op(lambda sl=sl: v.tensor_add(sl, sl, brep),
                              inc=("V", ("norm", 1, dc)))
                        if dc == 2:
                            wc(2)
                        if dc == 4:
                            wc(3)
                    return
                for dc in range(NDC):
                    Bd.wait(v, "X" if dc % 2 == 0 else "U", ("xld", pa, dc))
                    Bd.op(lambda dc=dc: v.tensor_copy(
                        xh[:, dc * 1024:(dc + 1) * 1024], xb[dc % 2]),
                        inc=("V", ("lncast", pa, dc)))
                Bd.wait(v, "A", ("es", pa))
                Bd.op(lambda: v.tensor_mul(tmp1, ms, ms))
                Bd.drain(v)
                Bd.op(lambda: v.tensor_sub(es, es, tmp1), inc=("V", ("sub", pa)))
                Bd.wait(v, "A", ("sqr", pa))
                Bd.op(lambda: v.scalar_tensor_tensor(
                    brep, ms, -1.0, arep, ALU.mult, ALU.mult),
                    inc=("V", ("brep", pa)))
                for fc in range(min(4, nfc)):
                    Bd.wait(v, "W" if fc % 2 == 0 else "W2", ("wld", pa, fc))
                    Bd.op(lambda fc=fc: v.tensor_copy(wbb[fc % 4][:, :], wfb[fc % 2]),
                          inc=("V", ("wcast", pa, fc)))
                Bd.wait(v, "T", ("stat", pa, NDC - 1))
                for dc in range(NDC):
                    sl = xh[:, dc * 1024:(dc + 1) * 1024]
                    Bd.op(lambda sl=sl: v.tensor_mul(sl, sl, arep))
                    Bd.drain(v)
                    Bd.op(lambda sl=sl: v.tensor_add(sl, sl, brep),
                          inc=("V", ("norm", pa, dc)))
                if pa == 0:
                    for fc in range(4, nfc):
                        Bd.wait(v, "W" if fc % 2 == 0 else "W2", ("wld", pa, fc))
                        Bd.wait(v, "T", ("qkv", pa, fc - 4))
                        Bd.op(lambda fc=fc: v.tensor_copy(wbb[fc % 4][:, :], wfb[fc % 2]),
                              inc=("V", ("wcast", pa, fc)))
                        dcB = fc - 4
                        if dcB < NDC:
                            Bd.wait(v, "X" if dcB % 2 == 0 else "U", ("xldB", dcB))
                            if dcB >= 2:
                                Bd.wait(v, "A", ("sqB", dcB - 2))
                                Bd.wait(v, "T", ("statB", dcB - 2))
                            Bd.op(lambda dcB=dcB: v.tensor_copy(
                                xBb[dcB % 2][:, :], xb[dcB % 2]),
                                inc=("V", ("xcB", dcB)))
                    Bd.wait(v, "A", ("es", 1))
                    Bd.op(lambda: v.tensor_mul(tmp1, ms, ms))
                    Bd.drain(v)
                    Bd.op(lambda: v.tensor_sub(es, es, tmp1), inc=("V", ("sub", 1)))
                    Bd.wait(v, "A", ("sqr", 1))
                    Bd.op(lambda: v.scalar_tensor_tensor(
                        brep, ms, -1.0, arep, ALU.mult, ALU.mult),
                        inc=("V", ("brep", 1)))

            def f_sc(s):
                if pa == 1:
                    return
                for dc in range(NDC):
                    Bd.wait(s, "V", ("lncast", pa, dc))
                    if dc >= 2:
                        Bd.wait(s, "T", ("stat", pa, dc - 2))
                    Bd.op(lambda dc=dc: s.activation(
                        sq[dc % 2][:, :], xh[:, dc * 1024:(dc + 1) * 1024], AF.Square),
                        inc=("A", ("sq", pa, dc)))
                Bd.wait(s, "T", ("stat", pa, NDC - 1))
                Bd.op(lambda: s.activation(ms, ps[:, 0:1024], AF.Copy, scale=RN),
                      inc=("A", ("ms", pa)))
                Bd.op(lambda: s.activation(es, ps[:, 1024:2048], AF.Copy, scale=RN),
                      inc=("A", ("es", pa)))
                Bd.wait(s, "V", ("sub", pa))
                Bd.op(lambda: s.activation(tmp1, es, AF.Ln, bias=eps_t))
                Bd.drain(s)
                Bd.op(lambda: s.activation(arep, tmp1, AF.Exp, scale=-0.5),
                      inc=("A", ("sqr", pa)))
                if pa == 0:
                    for fc in range(nfc):
                        kind, _, i = fcs[fc]
                        if kind == "q":
                            dst = Qt[:, i * 2048 + coloff:i * 2048 + coloff + 1024]
                        elif kind == "k":
                            dst = Kt[:, i * 2048 + coloff:i * 2048 + coloff + 1024]
                        else:
                            dst = Vt[:, i * 1024:(i + 1) * 1024]
                        Bd.wait(s, "T", ("qkv", pa, fc))
                        po = fc * 1024 if fc < 4 else (fc % 2) * 1024
                        Bd.op(lambda dst=dst, po=po: s.activation(
                            dst, ps[:, po:po + 1024], AF.Copy),
                            inc=("A", ("ev", pa, fc)))
                        dcB = fc - 2
                        if 0 <= dcB < NDC:
                            Bd.wait(s, "V", ("xcB", dcB))
                            if dcB >= 2:
                                Bd.wait(s, "T", ("statB", dcB - 2))
                            Bd.op(lambda dcB=dcB: s.activation(
                                sqB[dcB % 2][:, :], xBb[dcB % 2][:, :], AF.Square),
                                inc=("A", ("sqB", dcB)))
                        if fc == 20:
                            Bd.wait(s, "T", ("statB", NDC - 1))
                            Bd.op(lambda: s.activation(
                                ms, ps[:, 2048:3072], AF.Copy, scale=RN),
                                inc=("A", ("ms", 1)))
                            Bd.op(lambda: s.activation(
                                es, ps[:, 3072:4096], AF.Copy, scale=RN),
                                inc=("A", ("es", 1)))
                    Bd.wait(s, "V", ("sub", 1))
                    Bd.op(lambda: s.activation(tmp1, es, AF.Ln, bias=eps_t))
                    Bd.drain(s)
                    Bd.op(lambda: s.activation(arep, tmp1, AF.Exp, scale=-0.5),
                          inc=("A", ("sqr", 1)))

            def f_te(t):
                if pa == 1:
                    return
                Bd.wait(t, "V", ("ones",))
                for dc in range(NDC):
                    Bd.wait(t, "V", ("lncast", pa, dc))
                    Bd.wait(t, "A", ("sq", pa, dc))
                    st = (dc == 0)
                    sp = (dc == NDC - 1)
                    for pc in range(2):
                        Bd.op(lambda dc=dc, pc=pc, st=st, sp=sp: t.matmul(
                            ps[:, pc * 512:(pc + 1) * 512], ones[:, :],
                            xh[:, dc * 1024 + pc * 512:dc * 1024 + (pc + 1) * 512],
                            start=st, stop=sp, skip_group_check=True))
                    for pc in range(2):
                        Bd.op(lambda dc=dc, pc=pc, st=st, sp=sp: t.matmul(
                            ps[:, 1024 + pc * 512:1024 + (pc + 1) * 512], ones[:, :],
                            sq[dc % 2][:, pc * 512:(pc + 1) * 512],
                            start=st, stop=sp, skip_group_check=True),
                            inc=("T", ("stat", pa, dc)) if (pc == 1) else None)
                if pa == 0:
                    def statB_mm(dcB):
                        Bd.wait(t, "V", ("xcB", dcB))
                        Bd.wait(t, "A", ("sqB", dcB))
                        if dcB == 0:
                            Bd.wait(t, "A", ("ev", pa, 2))
                            Bd.wait(t, "A", ("ev", pa, 3))
                        st = (dcB == 0)
                        sp = (dcB == NDC - 1)
                        for pc in range(2):
                            Bd.op(lambda dcB=dcB, pc=pc, st=st, sp=sp: t.matmul(
                                ps[:, 2048 + pc * 512:2048 + (pc + 1) * 512],
                                ones[:, :],
                                xBb[dcB % 2][:, pc * 512:(pc + 1) * 512],
                                start=st, stop=sp, skip_group_check=True))
                        for pc in range(2):
                            Bd.op(lambda dcB=dcB, pc=pc, st=st, sp=sp: t.matmul(
                                ps[:, 3072 + pc * 512:3072 + (pc + 1) * 512],
                                ones[:, :],
                                sqB[dcB % 2][:, pc * 512:(pc + 1) * 512],
                                start=st, stop=sp, skip_group_check=True),
                                inc=("T", ("statB", dcB)) if pc == 1 else None)

                    for fc in range(4):
                        Bd.wait(t, "V", ("wcast", pa, fc))
                    Bd.wait(t, "A", ("es", pa))
                    for dc in range(NDC):
                        Bd.wait(t, "V", ("norm", pa, dc))
                        st = (dc == 0)
                        sp = (dc == NDC - 1)
                        for fc in range(4):
                            for pc in range(2):
                                Bd.op(lambda fc=fc, dc=dc, pc=pc, st=st, sp=sp: t.matmul(
                                    ps[:, fc * 1024 + pc * 512:fc * 1024 + (pc + 1) * 512],
                                    wbb[fc][:, dc * 128:(dc + 1) * 128],
                                    xh[:, dc * 1024 + pc * 512:dc * 1024 + (pc + 1) * 512],
                                    start=st, stop=sp, skip_group_check=True),
                                    inc=("T", ("qkv", pa, fc)) if (sp and pc == 1) else None)
                    for fc in range(4, nfc):
                        Bd.wait(t, "V", ("wcast", pa, fc))
                        if 4 <= fc < 20:
                            statB_mm(fc - 4)
                        Bd.wait(t, "A", ("ev", pa, fc - 4 if fc < 6 else fc - 2))
                        for dc in range(NDC):
                            st = (dc == 0)
                            sp = (dc == NDC - 1)
                            for pc in range(2):
                                Bd.op(lambda fc=fc, dc=dc, pc=pc, st=st, sp=sp: t.matmul(
                                    ps[:, (fc % 2) * 1024 + pc * 512:(fc % 2) * 1024 + (pc + 1) * 512],
                                    wbb[fc % 4][:, dc * 128:(dc + 1) * 128],
                                    xh[:, dc * 1024 + pc * 512:dc * 1024 + (pc + 1) * 512],
                                    start=st, stop=sp, skip_group_check=True),
                                    inc=("T", ("qkv", pa, fc)) if (sp and pc == 1) else None)

            def f_sync(s):
                for fc in range(nfc):
                    _, wap, i = fcs[fc]
                    if fc >= 2:
                        Bd.wait(s, "V", ("wcast", pa, fc - 2))
                    elif pa == 1:
                        Bd.wait(s, "V", ("wcast", 0, 30 + fc))
                    Bd.op(lambda wap=wap, i=i, fc=fc: s.dma_start(
                        out=wfb[fc % 2], in_=wap[i]),
                        inc=("W" if fc % 2 == 0 else "W2", ("wld", pa, fc)))
                if pa == 1:
                    Bd.wait(s, "V", ("wcast", 1, 14))
                    Bd.op(lambda: s.dma_start(out=wfb[0], in_=WO[0]),
                          inc=("W", ("wol", 0)))
                    Bd.wait(s, "V", ("wcast", 1, 15))
                    Bd.op(lambda: s.dma_start(out=wfb[1], in_=WO[1]),
                          inc=("W2", ("wol", 1)))

            return dict(sync=f_sync, gpsimd=f_gp, scalar=f_sc, vector=f_vec, tensor=f_te)

        def ph_att(e):
            CS_SLOT = [(0, 3072), (32, 3072), (0, 3584), (32, 3584)]

            def colsum_mm(t, h, j):
                Bd.wait(t, "V", ("rb", h, j))
                Bd.wait(t, "A", ("exp", h, j, 1))
                if j == 0:
                    Bd.wait_maybe(t, "A", ("csev", h - 1))
                    if h == 0:
                        Bd.wait(t, "A", ("ev", 1, 1))
                for pc in range(4):
                    row, col = CS_SLOT[pc]
                    tp = (0, 32) if row == 32 else None
                    Bd.op(lambda h=h, j=j, pc=pc, row=row, col=col, tp=tp: t.matmul(
                        ps[row:row + 1, col:col + 512],
                        rb[j % 4], Et[j % 4][:, pc * 512:(pc + 1) * 512],
                        start=(j == 0), stop=(j == 15), tile_position=tp,
                        skip_group_check=True),
                        inc=("T", ("cs", h, j)) if pc == 3 else None)

            STEPS = [(h, qc) for h in range(8) for qc in range(16)]

            def qkvb_mm(t, fc, dcs, first):
                if first:
                    Bd.wait(t, "V", ("wcast", 1, fc))
                    if fc == 0:
                        Bd.wait(t, "A", ("es", 1))
                    if fc >= 1:
                        Bd.wait(t, "A", ("ev", 1, fc - 1))
                for dc in dcs:
                    if fc == 0:
                        Bd.wait(t, "V", ("norm", 1, dc))
                    st = (dc == 0)
                    sp = (dc == NDC - 1)
                    for pc in range(2):
                        Bd.op(lambda fc=fc, dc=dc, pc=pc, st=st, sp=sp: t.matmul(
                            ps[:, 2048 + pc * 512:2048 + (pc + 1) * 512],
                            wbb[fc % 4][:, dc * 128:(dc + 1) * 128],
                            xh[:, dc * 1024 + pc * 512:dc * 1024 + (pc + 1) * 512],
                            start=st, stop=sp, skip_group_check=True),
                            inc=("T", ("qkv", 1, fc)) if (sp and pc == 1) else None)

            def f_te(t):
                Bd.wait(t, "V", ("wcast", 1, 0))
                Bd.wait(t, "V", ("wcast", 1, 1))
                Bd.wait(t, "A", ("es", 1))
                for dc in range(NDC):
                    Bd.wait(t, "V", ("norm", 1, dc))
                    st = (dc == 0)
                    sp = (dc == NDC - 1)
                    for fc in range(2):
                        for pc in range(2):
                            Bd.op(lambda fc=fc, dc=dc, pc=pc, st=st, sp=sp: t.matmul(
                                ps[:, 2048 + fc * 1024 + pc * 512:2048 + fc * 1024 + (pc + 1) * 512],
                                wbb[fc][:, dc * 128:(dc + 1) * 128],
                                xh[:, dc * 1024 + pc * 512:dc * 1024 + (pc + 1) * 512],
                                start=st, stop=sp, skip_group_check=True),
                                inc=("T", ("qkv", 1, fc)) if (sp and pc == 1) else None)
                for s_i, (h, qc) in enumerate(STEPS):
                    if s_i == 0:
                        Bd.wait(t, "A", ("es", 1))
                        Bd.wait(t, "A", ("ev", 0, 30))
                        Bd.wait(t, "A", ("ev", 0, 31))
                        Bd.wait(t, "A", ("ev", 1, 1))
                    elif qc == 0:
                        Bd.wait(t, "A", ("ev", 1, 2 * h + 1))
                    for hf in range(2):
                        u = 2 * s_i + hf
                        if u >= 2:
                            up = u - 2
                            hp, qp = STEPS[up // 2]
                            Bd.wait(t, "A", ("exp", hp, qp, up % 2))
                        po = (u % 2) * 1024
                        for pc in range(2):
                            Bd.op(lambda h=h, qc=qc, hf=hf, pc=pc, po=po: t.matmul(
                                ps[:, po + pc * 512:po + (pc + 1) * 512],
                                Qt[:, h * 2048 + qc * 128:h * 2048 + (qc + 1) * 128],
                                Kt[:, h * 2048 + hf * 1024 + pc * 512:h * 2048 + hf * 1024 + (pc + 1) * 512],
                                start=True, stop=True, skip_group_check=True),
                                inc=("T", ("sc", h, qc, hf)) if pc == 1 else None)
                    if s_i >= 2:
                        colsum_mm(t, *STEPS[s_i - 2])
                    if h <= 6:
                        fc = 2 * h + 2 + (0 if qc < 8 else 1)
                        dc0 = 2 * (qc % 8)
                        qkvb_mm(t, fc, (dc0, dc0 + 1), dc0 == 0)
                colsum_mm(t, *STEPS[126])
                colsum_mm(t, *STEPS[127])

            def cs_evac(s, hd):
                Bd.wait(s, "T", ("cs", hd, 15))
                if hd >= 1:
                    Bd.wait(s, "X", ("csd", hd - 1, 5))
                else:
                    Bd.wait(s, "V", ("brep", 1))
                for pc in range(4):
                    row, col = CS_SLOT[pc]
                    Bd.op(lambda pc=pc, row=row, col=col: s.activation(
                        cstage[row:row + 1, pc * 512:(pc + 1) * 512],
                        ps[row:row + 1, col:col + 512], AF.Copy),
                        inc=("A", ("csev", hd)) if pc == 3 else None)

            def qkvb_ev(s, fc):
                i = fc // 2
                dst = (Qt if fc % 2 == 0 else Kt)[:, i * 2048 + 1024:i * 2048 + 2048]
                Bd.wait(s, "T", ("qkv", 1, fc))
                po = 3072 if fc == 1 else 2048
                Bd.op(lambda dst=dst, po=po: s.activation(
                    dst, ps[:, po:po + 1024], AF.Copy),
                    inc=("A", ("ev", 1, fc)))

            def f_sc(s):
                qkvb_ev(s, 0)
                qkvb_ev(s, 1)
                for s_i, (h, qc) in enumerate(STEPS):
                    if qc == 0 and h >= 1:
                        qkvb_ev(s, 2 * h + 1)
                    if qc == 8 and h <= 6:
                        qkvb_ev(s, 2 * h + 2)
                    for hf in range(2):
                        u = 2 * s_i + hf
                        Bd.wait(s, "T", ("sc", h, qc, hf))
                        if hf == 0 and s_i >= 4:
                            Bd.wait(s, "V", ("rb", *STEPS[s_i - 4]))
                            Bd.wait(s, "T", ("cs", *STEPS[s_i - 4]))
                        zdst = z1[u % 8]
                        po = (u % 2) * 1024
                        Bd.op(lambda h=h, qc=qc, hf=hf, zdst=zdst, po=po: s.activation(
                            Et[(h * 16 + qc) % 4][:, hf * 1024:(hf + 1) * 1024],
                            ps[:, po:po + 1024],
                            AF.Exp, scale=ISQ, accum_out=zdst),
                            inc=("A", ("exp", h, qc, hf)))
                    if s_i >= 2 and STEPS[s_i - 2][1] == 15:
                        cs_evac(s, STEPS[s_i - 2][0])
                cs_evac(s, 7)
                Bd.wait(s, "W", ("wol", 0))
                Bd.wait(s, "T", ("qkv", 1, 12))
                Bd.op(lambda: s.activation(wbb[0][:, :], wfb[0], AF.Copy),
                      inc=("A", ("wosc", 0)))
                Bd.wait(s, "W2", ("wol", 1))
                Bd.wait(s, "T", ("qkv", 1, 13))
                Bd.op(lambda: s.activation(wbb[1][:, :], wfb[1], AF.Copy),
                      inc=("A", ("wosc", 1)))

            def f_vec(v):
                for s_i, (h, qc) in enumerate(STEPS):
                    Bd.wait(v, "A", ("exp", h, qc, 1))
                    z0 = z1[(2 * s_i) % 8]
                    z1b = z1[(2 * s_i + 1) % 8]
                    Bd.op(lambda z0=z0, z1b=z1b: v.tensor_add(zf, z0, z1b))
                    Bd.drain(v)
                    Bd.op(lambda: v.reciprocal(rt, zf))
                    Bd.drain(v)
                    p = s_i % 4
                    Bd.op(lambda p=p: v.tensor_copy(rb[p], rt),
                          inc=("V", ("rb", h, qc)))
                    if qc == 2 and h <= 5:
                        for fc in (2 * h + 4, 2 * h + 5):
                            Bd.wait(v, "W" if fc % 2 == 0 else "W2", ("wld", 1, fc))
                            Bd.wait(v, "T", ("qkv", 1, fc - 4))
                            Bd.op(lambda fc=fc: v.tensor_copy(
                                wbb[fc % 4][:, :], wfb[fc % 2]),
                                inc=("V", ("wcast", 1, fc)))
                    if s_i >= 19 and s_i % 16 == 3:
                        hd = (s_i - 19) // 16
                        Bd.wait(v, "R", ("csb", hd))
                        sl = Vt[:, hd * 1024:(hd + 1) * 1024]
                        Bd.op(lambda sl=sl, hd=hd: v.tensor_mul(sl, sl, csb_at[hd % 2]),
                              inc=("V", ("att", hd)))
                Bd.wait(v, "R", ("csb", 7))
                sl7 = Vt[:, 7 * 1024:8 * 1024]
                Bd.op(lambda: v.tensor_mul(sl7, sl7, csb_at[7 % 2]),
                      inc=("V", ("att", 7)))
                Bd.wait(v, "X", ("cci1",))
                Bd.op(lambda: v.tensor_sub(sub_sb[0:4, :], ccs_sb[0:4, :],
                                           cci_sb[0:4, :]),
                      inc=("V", ("ccsub1",)))
                for l in range(8, 12):
                    Bd.wait(v, "R", ("csb", l))
                    sl = Vt[:, l * 1024:(l + 1) * 1024]
                    Bd.op(lambda sl=sl, l=l: v.tensor_mul(sl, sl, csb[l % 4]),
                          inc=("V", ("att", l)))

            def f_gp(g):
                CSD_SRC = [(0, 0), (32, 512), (0, 1024), (32, 1536)]
                for h in range(8):
                    Bd.wait(g, "A", ("csev", h))
                    for i, (row, col) in enumerate(CSD_SRC):
                        Bd.op(lambda h=h, row=row, col=col: g.dma_start(
                            out=cs_own[h:h + 1, col:col + 512],
                            in_=cstage[row:row + 1, col:col + 512]),
                            inc=("X", ("csd", h, i)))
                    Bd.op(lambda h=h: g.dma_start(
                        out=cc_in[h:h + 1, 0:512], in_=cstage[0:1, 1024:1536]),
                        inc=("X", ("csd", h, 4)))
                    Bd.op(lambda h=h: g.dma_start(
                        out=cc_in[h:h + 1, 512:1024], in_=cstage[32:33, 1536:2048]),
                        inc=("X", ("csd", h, 5)))
                    Bd.wait(g, "X", ("csd", h, 5))
                    if h == 0:
                        Bd.wait(g, "V", ("norm", 1, NDC - 1))
                    if h >= 1:
                        Bd.wait(g, "V", ("att", h - 1))
                    Bd.op(lambda h=h: g.dma_start(
                        out=csb_at[h % 2],
                        in_=cs_own[h:h + 1, 0:1024].to_broadcast((128, 1024))),
                        inc=("R", ("csb", h)))
                    if h == 3:
                        Bd.wait(g, "X", ("csd", 3, 5))
                        Bd.op(lambda: g.collective_compute(
                            "AllReduce", ALU.add,
                            replica_groups=[[0, 1], [2, 3], [4, 5], [6, 7]],
                            ins=[cc_in[0:4, :].opt()], outs=[cc_sum[0:4, :].opt()]),
                            inc=("C", ("ag1",)))
                    if h == 7:
                        Bd.wait(g, "X", ("csd", 7, 5))
                        Bd.op(lambda: g.collective_compute(
                            "AllReduce", ALU.add,
                            replica_groups=[[0, 1], [2, 3], [4, 5], [6, 7]],
                            ins=[cc_in[4:8, :].opt()], outs=[cc_sum[4:8, :].opt()]),
                            inc=("C", ("ag2",)))
                Bd.wait(g, "T", ("cs", 7, 15))
                Bd.wait(g, "C", ("ag1",))
                Bd.op(lambda: g.dma_start(out=ccs_sb[0:4, :], in_=cc_sum[0:4, :]),
                      inc=("X", ("ccs1",)))
                Bd.op(lambda: g.dma_start(out=cci_sb[0:4, :], in_=cc_in[0:4, :]),
                      inc=("X", ("cci1",)))
                Bd.wait(g, "V", ("ccsub1",))
                Bd.op(lambda: g.dma_start(out=cs_part[0:4, :], in_=sub_sb[0:4, :]),
                      inc=("X", ("csp1",)))
                for l in range(8, 12):
                    Bd.wait(g, "X", ("csp1",))
                    Bd.wait(g, "V", ("att", l - 1))
                    Bd.op(lambda l=l: g.dma_start(
                        out=csb[l % 4],
                        in_=cs_part[l - 8:l - 7, :].to_broadcast((128, 1024))),
                        inc=("R", ("csb", l)))

            return dict(gpsimd=f_gp, scalar=f_sc, vector=f_vec, tensor=f_te)

        def ph_wo(e):
            def f_gp(g):
                Bd.wait(g, "C", ("ag2",))
                Bd.op(lambda: g.dma_start(out=ccs_h2, in_=cc_sum[4:8, :]),
                      inc=("X", ("ccs",)))
                Bd.op(lambda: g.dma_start(out=cci_h2, in_=cc_in[4:8, :]),
                      inc=("X", ("cci",)))
                Bd.wait(g, "V", ("ccsub",))
                Bd.op(lambda: g.dma_start(out=cs_part[4:8, :], in_=sub_h2),
                      inc=("X", ("csp",)))
                if debug:
                    Bd.op(lambda: g.dma_start(out=dbg["cs"], in_=cs_own),
                          inc=("X", ("dbgcs",)))
                    Bd.wait(g, "X", ("csp",))
                    Bd.op(lambda: g.dma_start(out=dbg["csp"], in_=cs_part),
                          inc=("X", ("dbgcsp",)))
                for l in range(12, 16):
                    Bd.wait(g, "X", ("csp",))
                    Bd.wait(g, "V", ("att", l - 1))
                    Bd.op(lambda l=l: g.dma_start(
                        out=csb[l % 4],
                        in_=cs_part[l - 8:l - 7, :].to_broadcast((128, 1024))),
                        inc=("R", ("csb", l)))
                for oc in range(NDC):
                    if oc >= 2:
                        Bd.wait(g, "V", ("x1f", oc - 2))
                    Bd.op(lambda oc=oc: g.dma_start(
                        out=xb[oc % 2], in_=xT[oc * 128:(oc + 1) * 128, 0:1024]),
                        inc=("X" if oc % 2 == 0 else "U", ("xre", oc)))

            def f_vec(v):
                Bd.wait(v, "X", ("cci",))
                Bd.op(lambda: v.tensor_sub(sub_h2, ccs_h2, cci_h2),
                      inc=("V", ("ccsub",)))
                for l in range(12, 16):
                    Bd.wait(v, "R", ("csb", l))
                    sl = Vt[:, l * 1024:(l + 1) * 1024]
                    Bd.op(lambda sl=sl, l=l: v.tensor_mul(sl, sl, csb[l % 4]),
                          inc=("V", ("att", l)))
                Bd.wait(v, "A", ("es1",))
                Bd.op(lambda: v.tensor_mul(tmp1, ms, ms))
                Bd.drain(v)
                Bd.op(lambda: v.tensor_sub(es, es, tmp1), inc=("V", ("sub1",)))
                Bd.wait(v, "A", ("sqr1",))
                Bd.op(lambda: v.scalar_tensor_tensor(
                    brep, ms, -1.0, arep, ALU.mult, ALU.mult), inc=("V", ("brep1",)))
                Bd.wait(v, "T", ("wost", NDC - 1))
                for oc in range(NDC):
                    Bd.wait(v, "X" if oc % 2 == 0 else "U", ("xre", oc))
                    if oc >= 4:
                        Bd.wait(v, "A", ("x1c", oc - 4))
                        Bd.wait_maybe(v, YSEM[oc % 4], ("x1d", oc - 4))
                        Bd.wait_maybe(v, XOSEM[oc % 4], ("xo", oc - 4))
                    t1 = gtmp[oc % 2]
                    abf = attn_bf[:, oc * 1024:(oc + 1) * 1024]
                    Bd.op(lambda t1=t1, abf=abf: v.tensor_mul(t1[:, :], abf, arep))
                    Bd.drain(v)
                    Bd.op(lambda t1=t1: v.tensor_add(t1[:, :], t1[:, :], brep))
                    Bd.drain(v)
                    Bd.op(lambda t1=t1, oc=oc: v.tensor_add(
                        x1f[oc % 4], xb[oc % 2], t1[:, :]),
                        inc=("V", ("x1f", oc)))

            def wo_stat_mm(t, oc):
                Bd.wait(t, "A", ("wosq", oc))
                st = (oc == 0)
                sp = (oc == NDC - 1)
                for pc in range(2):
                    Bd.op(lambda oc=oc, pc=pc, st=st, sp=sp: t.matmul(
                        ps[:, 2048 + pc * 512:2048 + (pc + 1) * 512], ones[:, :],
                        attn_bf[:, oc * 1024 + pc * 512:oc * 1024 + (pc + 1) * 512],
                        start=st, stop=sp, skip_group_check=True))
                for pc in range(2):
                    Bd.op(lambda oc=oc, pc=pc, st=st, sp=sp: t.matmul(
                        ps[:, 3072 + pc * 512:3072 + (pc + 1) * 512], ones[:, :],
                        sq[oc % 2][:, pc * 512:(pc + 1) * 512],
                        start=st, stop=sp, skip_group_check=True),
                        inc=("T", ("wost", oc)) if pc == 1 else None)

            def f_te(t):
                for oc in range(NDC):
                    Bd.wait(t, "A", ("wosc", oc))
                    if oc >= 2:
                        Bd.wait(t, "A", ("woev", oc - 2))
                    po = (oc % 2) * 1024
                    for fc in range(16):
                        if oc == 0:
                            Bd.wait(t, "V", ("att", fc))
                        st = (fc == 0)
                        sp = (fc == 15)
                        for pc in range(2):
                            Bd.op(lambda oc=oc, fc=fc, pc=pc, st=st, sp=sp, po=po: t.matmul(
                                ps[:, po + pc * 512:po + (pc + 1) * 512],
                                wbb[oc % 4][:, fc * 128:(fc + 1) * 128],
                                Vt[:, fc * 1024 + pc * 512:fc * 1024 + (pc + 1) * 512],
                                start=st, stop=sp, skip_group_check=True),
                                inc=("T", ("wom", oc)) if (sp and pc == 1) else None)
                    if oc >= 1:
                        wo_stat_mm(t, oc - 1)
                wo_stat_mm(t, NDC - 1)
                Bd.wait(t, "A", ("woev", NDC - 1))
                for oc in range(NDC):
                    Bd.wait(t, "A", ("l2sq", oc))
                    st = (oc == 0)
                    sp = (oc == NDC - 1)
                    for pc in range(2):
                        Bd.op(lambda oc=oc, pc=pc, st=st, sp=sp: t.matmul(
                            ps[:, pc * 512:(pc + 1) * 512], ones[:, :],
                            x1bf[:, oc * 1024 + pc * 512:oc * 1024 + (pc + 1) * 512],
                            start=st, stop=sp, skip_group_check=True))
                    for pc in range(2):
                        Bd.op(lambda oc=oc, pc=pc, st=st, sp=sp: t.matmul(
                            ps[:, 1024 + pc * 512:1024 + (pc + 1) * 512], ones[:, :],
                            sq[oc % 2][:, pc * 512:(pc + 1) * 512],
                            start=st, stop=sp, skip_group_check=True),
                            inc=("T", ("l2st", oc)) if pc == 1 else None)

            def f_sc(s):
                for oc in range(NDC):
                    Bd.wait(s, "T", ("wom", oc))
                    if oc >= 2:
                        Bd.wait(s, "T", ("wost", oc - 2))
                    abf = attn_bf[:, oc * 1024:(oc + 1) * 1024]
                    Bd.op(lambda abf=abf, oc=oc: s.activation(
                        abf, ps[:, (oc % 2) * 1024:(oc % 2) * 1024 + 1024], AF.Copy),
                        inc=("A", ("woev", oc)))
                    Bd.drain(s)
                    Bd.op(lambda abf=abf, oc=oc: s.activation(
                        sq[oc % 2][:, :], abf, AF.Square),
                        inc=("A", ("wosq", oc)))
                    if oc + 2 < NDC:
                        Bd.wait(s, "W" if oc % 2 == 0 else "W2", ("wol", oc + 2))
                        if oc + 2 >= 4:
                            Bd.wait(s, "T", ("wom", oc - 2))
                        Bd.op(lambda oc=oc: s.activation(
                            wbb[(oc + 2) % 4][:, :], wfb[oc % 2], AF.Copy),
                            inc=("A", ("wosc", oc + 2)))
                Bd.wait(s, "T", ("wost", NDC - 1))
                Bd.wait(s, "X", ("csd", 7, 5))
                Bd.op(lambda: s.activation(ms, ps[:, 2048:3072], AF.Copy, scale=RN),
                      inc=("A", ("ms1",)))
                Bd.op(lambda: s.activation(es, ps[:, 3072:4096], AF.Copy, scale=RN),
                      inc=("A", ("es1",)))
                Bd.wait(s, "V", ("sub1",))
                Bd.op(lambda: s.activation(tmp1, es, AF.Ln, bias=eps_t))
                Bd.drain(s)
                Bd.op(lambda: s.activation(arep, tmp1, AF.Exp, scale=-0.5),
                      inc=("A", ("sqr1",)))
                for oc in range(NDC):
                    Bd.wait(s, "V", ("x1f", oc))
                    Bd.op(lambda oc=oc: s.activation(
                        x1bf[:, oc * 1024:(oc + 1) * 1024], x1f[oc % 4], AF.Copy),
                        inc=("A", ("x1c", oc)))
                    if oc < LXD:
                        Bd.op(lambda oc=oc: s.dma_start(
                            out=x1b[oc * 128:(oc + 1) * 128, :], in_=x1f[oc % 4]),
                            inc=(YSEM[oc % 4], ("x1d", oc)))
                    if oc >= RBN or debug:
                        Bd.op(lambda oc=oc: s.dma_start(
                            out=outT[oc * 128:(oc + 1) * 128, :], in_=x1f[oc % 4]),
                            inc=(XOSEM[oc % 4], ("xo", oc)))
                    Bd.drain(s)
                    if oc >= 2:
                        Bd.wait(s, "T", ("l2st", oc - 2))
                    Bd.op(lambda oc=oc: s.activation(
                        sq[oc % 2][:, :], x1bf[:, oc * 1024:(oc + 1) * 1024], AF.Square),
                        inc=("A", ("l2sq", oc)))

            def f_sync(s):
                for oc in range(2, NDC):
                    Bd.wait(s, "A", ("wosc", oc - 2))
                    Bd.op(lambda oc=oc: s.dma_start(out=wfb[oc % 2], in_=WO[oc]),
                          inc=("W" if oc % 2 == 0 else "W2", ("wol", oc)))

            return dict(sync=f_sync, gpsimd=f_gp, scalar=f_sc, vector=f_vec, tensor=f_te)

        def ph_ln2(e):
            def f_sc(s):
                Bd.wait(s, "T", ("l2st", NDC - 1))
                Bd.op(lambda: s.activation(ms, ps[:, 0:1024], AF.Copy, scale=RN),
                      inc=("A", ("ms2",)))
                Bd.op(lambda: s.activation(es, ps[:, 1024:2048], AF.Copy, scale=RN),
                      inc=("A", ("es2",)))
                Bd.wait(s, "V", ("sub2",))
                Bd.op(lambda: s.activation(tmp1, es, AF.Ln, bias=eps_t))
                Bd.drain(s)
                Bd.op(lambda: s.activation(arep, tmp1, AF.Exp, scale=-0.5),
                      inc=("A", ("sqr2",)))
                Bd.wait(s, "S", ("dsl",))
                Bd.wait(s, "W", ("gl", 0))
                Bd.wait(s, "T", ("wom", NDC - 1))
                Bd.op(lambda: s.activation(gwb[0][:, :], gwf[0], AF.Copy),
                      inc=("A", ("gcs", 0)))
                Bd.wait(s, "W3", ("ul", 0))
                Bd.op(lambda: s.activation(uwb[0][:, :], uwf[0], AF.Copy),
                      inc=("A", ("ucs", 0)))
                Bd.wait(s, "W2", ("gl", 1))
                Bd.op(lambda: s.activation(gwb[1][:, :], gwf[1], AF.Copy),
                      inc=("A", ("gcs", 1)))
                Bd.wait(s, "W4", ("ul", 1))
                Bd.op(lambda: s.activation(uwb[1][:, :], uwf[1], AF.Copy),
                      inc=("A", ("ucs", 1)))

            def f_vec(v):
                Bd.wait(v, "A", ("es2",))
                Bd.op(lambda: v.tensor_mul(tmp1, ms, ms))
                Bd.drain(v)
                Bd.op(lambda: v.tensor_sub(es, es, tmp1), inc=("V", ("sub2",)))
                Bd.wait(v, "A", ("sqr2",))
                Bd.op(lambda: v.scalar_tensor_tensor(
                    brep, ms, -1.0, arep, ALU.mult, ALU.mult), inc=("V", ("brep2",)))
                Bd.wait(v, "T", ("l2st", NDC - 1))
                for oc in range(NDC):
                    sl = x1bf[:, oc * 1024:(oc + 1) * 1024]
                    Bd.op(lambda sl=sl: v.tensor_mul(sl, sl, arep))
                    Bd.drain(v)
                    Bd.op(lambda sl=sl: v.tensor_add(sl, sl, brep),
                          inc=("V", ("ln2", oc)))

            return dict(scalar=f_sc, vector=f_vec)

        def ph_gu(e):
            def f_sync(s):
                for ic in range(NIC):
                    if ic == 0:
                        Bd.wait(s, "A", ("wosc", NDC - 2))
                    if ic == 1:
                        Bd.wait(s, "A", ("wosc", NDC - 1))
                    if ic == 2:
                        Bd.wait(s, "A", ("gcs", 0))
                        Bd.wait(s, "A", ("ucs", 0))
                    if ic == 3:
                        Bd.wait(s, "A", ("gcs", 1))
                        Bd.wait(s, "A", ("ucs", 1))
                    if ic >= 4:
                        Bd.wait(s, "V", ("gcast", ic - 2))
                        Bd.wait(s, "V", ("ucast", ic - 2))
                    if ic == 0:
                        Bd.wait(s, "T", ("wom", NDC - 1))
                    Bd.op(lambda ic=ic: s.dma_start(out=gwf[ic % 2], in_=GW[ic]),
                          inc=("W" if ic % 2 == 0 else "W2", ("gl", ic)))
                    Bd.op(lambda ic=ic: s.dma_start(out=uwf[ic % 2], in_=UW[ic]),
                          inc=("W3" if ic % 2 == 0 else "W4", ("ul", ic)))
                Bd.op(lambda: s.dma_start(out=dwf_pre, in_=DW[0]),
                      inc=("W5", ("dl", 0)))

            def f_vec(v):
                for ic in range(NIC):
                    if ic >= 2:
                        Bd.wait(v, "W" if ic % 2 == 0 else "W2", ("gl", ic))
                        Bd.wait(v, "T", ("gm", ic - 2))
                        Bd.op(lambda ic=ic: v.tensor_copy(gwb[ic % 2][:, :], gwf[ic % 2]),
                              inc=("V", ("gcast", ic)))
                        Bd.wait(v, "W3" if ic % 2 == 0 else "W4", ("ul", ic))
                        Bd.wait(v, "T", ("um", ic - 2))
                        Bd.op(lambda ic=ic: v.tensor_copy(uwb[ic % 2][:, :], uwf[ic % 2]),
                              inc=("V", ("ucast", ic)))
                    if ic >= 1:
                        j = ic - 1
                        if j == 0:
                            Bd.wait(v, "T", ("gm", 0))
                        if j == 20:
                            for i in range(4):
                                if LX1D[i]:
                                    Bd.wait(v, YSEM[i], ("x1d", LX1D[i][-1]))
                                if LXO[i]:
                                    Bd.wait(v, XOSEM[i], ("xo", LXO[i][-1]))
                        Bd.wait(v, "A", ("gev", j))
                        Bd.wait(v, "A", ("uev", j))
                        Bd.op(lambda j=j: v.tensor_mul(
                            ht[:, j * 1024:(j + 1) * 1024], gtmp[j % 2][:, :],
                            utmp[j % 2][:, :]),
                            inc=("V", ("h", j)))
                j = NIC - 1
                Bd.wait(v, "A", ("gev", j))
                Bd.wait(v, "A", ("uev", j))
                Bd.op(lambda j=j: v.tensor_mul(
                    ht[:, j * 1024:(j + 1) * 1024], gtmp[j % 2][:, :], utmp[j % 2][:, :]),
                    inc=("V", ("h", j)))

            def f_te(t):
                for ic in range(NIC):
                    if ic < 2:
                        Bd.wait(t, "A", ("gcs", ic))
                    else:
                        Bd.wait(t, "V", ("gcast", ic))
                        Bd.wait(t, "A", ("gev", ic - 2))
                    if ic == 0:
                        Bd.wait(t, "A", ("es2",))
                        Bd.wait(t, "A", ("es1",))
                    po = (ic % 2) * 2048
                    for dc in range(NDC):
                        if ic == 0:
                            Bd.wait(t, "V", ("ln2", dc))
                        st = (dc == 0)
                        sp = (dc == NDC - 1)
                        for pc in range(2):
                            Bd.op(lambda ic=ic, dc=dc, pc=pc, st=st, sp=sp, po=po: t.matmul(
                                ps[:, po + pc * 512:po + (pc + 1) * 512],
                                gwb[ic % 2][:, dc * 128:(dc + 1) * 128],
                                x1bf[:, dc * 1024 + pc * 512:dc * 1024 + (pc + 1) * 512],
                                start=st, stop=sp, skip_group_check=True),
                                inc=("T", ("gm", ic)) if (sp and pc == 1) else None)
                    if ic < 2:
                        Bd.wait(t, "A", ("ucs", ic))
                    else:
                        Bd.wait(t, "V", ("ucast", ic))
                        Bd.wait(t, "A", ("uev", ic - 2))
                    for dc in range(NDC):
                        st = (dc == 0)
                        sp = (dc == NDC - 1)
                        for pc in range(2):
                            Bd.op(lambda ic=ic, dc=dc, pc=pc, st=st, sp=sp, po=po: t.matmul(
                                ps[:, po + 1024 + pc * 512:po + 1024 + (pc + 1) * 512],
                                uwb[ic % 2][:, dc * 128:(dc + 1) * 128],
                                x1bf[:, dc * 1024 + pc * 512:dc * 1024 + (pc + 1) * 512],
                                start=st, stop=sp, skip_group_check=True),
                                inc=("T", ("um", ic)) if (sp and pc == 1) else None)

            def f_sc(s):
                for ic in range(NIC):
                    Bd.wait(s, "T", ("gm", ic))
                    if ic >= 2:
                        Bd.wait(s, "V", ("h", ic - 2))
                    po = (ic % 2) * 2048
                    Bd.op(lambda ic=ic, po=po: s.activation(
                        gtmp[ic % 2][:, :], ps[:, po:po + 1024], AF.Silu,
                        scale=gs_t[:, ic:ic + 1]),
                        inc=("A", ("gev", ic)))
                    Bd.wait(s, "T", ("um", ic))
                    Bd.op(lambda ic=ic, po=po: s.activation(
                        utmp[ic % 2][:, :], ps[:, po + 1024:po + 2048], AF.Copy,
                        scale=us_t[:, ic:ic + 1]),
                        inc=("A", ("uev", ic)))

            return dict(sync=f_sync, scalar=f_sc, vector=f_vec, tensor=f_te)

        def ph_dn(e):
            def f_sync(s):
                for oc in range(1, NDC):
                    if oc >= 2:
                        Bd.wait(s, "V", ("dcast", oc - 1))
                    else:
                        Bd.wait(s, "V", ("gcast", NIC - 1))
                        Bd.wait(s, "T", ("gm", NIC - 1))
                    Bd.op(lambda oc=oc: s.dma_start(out=dwf, in_=DW[oc]),
                          inc=("W5" if oc % 2 == 0 else "W6", ("dl", oc)))
                Bd.wait(s, "Y", ("x1d", LX1D[0][-1]))
                Bd.wait(s, "Y2", ("x1d", LX1D[1][-1]))
                Bd.wait(s, "W", ("gl", NIC - 2))
                Bd.wait(s, "W2", ("gl", NIC - 1))
                for j in range(RBN):
                    if j < 2:
                        Bd.wait(s, "V", ("dcast", 0))
                    else:
                        if j == 4:
                            Bd.wait(s, "W5", ("dl", NDC - 2))
                        if j == 5:
                            Bd.wait(s, "W6", ("dl", NDC - 1))
                        Bd.wait(s, "V", ("dcast", NDC - 1))
                    Bd.op(lambda j=j: s.dma_start(
                        out=xrb[j], in_=x1b[j * 128:(j + 1) * 128, :]),
                        inc=(RB_SEM[j], ("xrb", j)))

            def f_vec(v):
                for oc in range(NDC):
                    Bd.wait(v, "W5" if oc % 2 == 0 else "W6", ("dl", oc))
                    if oc >= 2:
                        Bd.wait(v, "T", ("dm", oc - 2))
                    elif oc == 0:
                        Bd.wait(v, "T", ("gm", NIC - 1))
                        Bd.wait(v, "T", ("um", NIC - 1))
                    src = dwf_pre if oc == 0 else dwf
                    Bd.op(lambda oc=oc, src=src: v.tensor_copy(dwb[oc % 2][:, :], src),
                          inc=("V", ("dcast", oc)))

            def dn_stat_mm(t, oc):
                Bd.wait(t, "A", ("dsq", oc))
                st = (oc == 0)
                sp = (oc == NDC - 1)
                for pc in range(2):
                    Bd.op(lambda oc=oc, pc=pc, st=st, sp=sp: t.matmul(
                        ps[:, 2048 + pc * 512:2048 + (pc + 1) * 512], ones[:, :],
                        ffn_bf[:, oc * 1024 + pc * 512:oc * 1024 + (pc + 1) * 512],
                        start=st, stop=sp, skip_group_check=True))
                for pc in range(2):
                    Bd.op(lambda oc=oc, pc=pc, st=st, sp=sp: t.matmul(
                        ps[:, 3072 + pc * 512:3072 + (pc + 1) * 512], ones[:, :],
                        sq[oc % 2][:, pc * 512:(pc + 1) * 512],
                        start=st, stop=sp, skip_group_check=True),
                        inc=("T", ("dst", oc)) if pc == 1 else None)

            def f_te(t):
                for oc in range(NDC):
                    Bd.wait(t, "V", ("dcast", oc))
                    if oc == 0:
                        Bd.wait(t, "V", ("h", NIC - 1))
                    if oc >= 2:
                        Bd.wait(t, "A", ("dev", oc - 2))
                    po = (oc % 2) * 1024
                    for ic in range(NIC):
                        st = (ic == 0)
                        sp = (ic == NIC - 1)
                        for pc in range(2):
                            Bd.op(lambda oc=oc, ic=ic, pc=pc, st=st, sp=sp, po=po: t.matmul(
                                ps[:, po + pc * 512:po + (pc + 1) * 512],
                                dwb[oc % 2][:, ic * 128:(ic + 1) * 128],
                                ht[:, ic * 1024 + pc * 512:ic * 1024 + (pc + 1) * 512],
                                start=st, stop=sp, skip_group_check=True),
                                inc=("T", ("dm", oc)) if (sp and pc == 1) else None)
                    if oc >= 1:
                        dn_stat_mm(t, oc - 1)
                dn_stat_mm(t, NDC - 1)

            def f_sc(s):
                Bd.wait(s, "S", ("dsl",))
                for oc in range(NDC):
                    Bd.wait(s, "T", ("dm", oc))
                    if oc >= 2:
                        Bd.wait(s, "T", ("dst", oc - 2))
                    fbf = ffn_bf[:, oc * 1024:(oc + 1) * 1024]
                    Bd.op(lambda fbf=fbf, oc=oc: s.activation(
                        fbf, ps[:, (oc % 2) * 1024:(oc % 2) * 1024 + 1024], AF.Copy,
                        scale=ds_t[:, oc:oc + 1]),
                        inc=("A", ("dev", oc)))
                    Bd.drain(s)
                    Bd.op(lambda fbf=fbf, oc=oc: s.activation(
                        sq[oc % 2][:, :], fbf, AF.Square),
                        inc=("A", ("dsq", oc)))

            return dict(sync=f_sync, scalar=f_sc, vector=f_vec, tensor=f_te)

        def ph_fi(e):
            def f_sc(s):
                Bd.wait(s, "T", ("dst", NDC - 1))
                Bd.op(lambda: s.activation(ms, ps[:, 2048:3072], AF.Copy, scale=RN),
                      inc=("A", ("ms3",)))
                Bd.op(lambda: s.activation(es, ps[:, 3072:4096], AF.Copy, scale=RN),
                      inc=("A", ("es3",)))
                Bd.wait(s, "V", ("sub3",))
                Bd.op(lambda: s.activation(tmp1, es, AF.Ln, bias=eps_t))
                Bd.drain(s)
                Bd.op(lambda: s.activation(arep, tmp1, AF.Exp, scale=-0.5),
                      inc=("A", ("sqr3",)))


            def f_vec(v):
                Bd.wait(v, "A", ("es3",))
                Bd.op(lambda: v.tensor_mul(tmp1, ms, ms))
                Bd.drain(v)
                Bd.op(lambda: v.tensor_sub(es, es, tmp1), inc=("V", ("sub3",)))
                Bd.wait(v, "A", ("sqr3",))
                Bd.op(lambda: v.scalar_tensor_tensor(
                    brep, ms, -1.0, arep, ALU.mult, ALU.mult), inc=("V", ("brep3",)))
                Bd.wait(v, "T", ("dst", NDC - 1))
                for i, oc in enumerate(FI_SEQ):
                    if i >= 4:
                        poc = FI_SEQ[i - 4]
                        if poc < RBN:
                            Bd.wait(v, RB_SEM[poc], ("wout", poc))
                        else:
                            Bd.wait(v, ["O", "O2", "O3", "O4"][FI_SLOT[poc]],
                                    ("out", poc))
                    t1 = sq[i % 2]
                    fbf = ffn_bf[:, oc * 1024:(oc + 1) * 1024]
                    Bd.op(lambda t1=t1, fbf=fbf: v.tensor_mul(t1[:, :], fbf, arep))
                    Bd.drain(v)
                    Bd.op(lambda t1=t1, oc=oc: v.tensor_add(
                        ostage[FI_SLOT[oc]], t1[:, :], brep),
                        inc=("V", ("ost", oc)))
                    if oc < RBN:
                        Bd.wait(v, RB_SEM[oc], ("xrb", oc))
                        Bd.drain(v)
                        Bd.op(lambda oc=oc: v.tensor_add(
                            ostage[FI_SLOT[oc]], ostage[FI_SLOT[oc]], xrb[oc]),
                            inc=("V", ("fin", oc)))

            def f_sync(s):
                for oc in range(RBN):
                    Bd.wait(s, "V", ("fin", oc))
                    Bd.op(lambda oc=oc: s.dma_start(
                        out=outT[oc * 128:(oc + 1) * 128, :],
                        in_=ostage[FI_SLOT[oc]]),
                        inc=(RB_SEM[oc], ("wout", oc)))

            def f_gp(g):
                for i in range(4):
                    if LXO[i]:
                        Bd.wait(g, XOSEM[i], ("xo", LXO[i][-1]))
                for oc in range(RBN, NDC):
                    Bd.wait(g, "V", ("ost", oc))
                    Bd.op(lambda oc=oc: g.dma_start(
                        out=outT[oc * 128:(oc + 1) * 128, :],
                        in_=ostage[FI_SLOT[oc]],
                        accum_op=ALU.add),
                        inc=(["O", "O2", "O3", "O4"][FI_SLOT[oc]], ("out", oc)))
                if debug:
                    for i in range(4):
                        if LX1D[i]:
                            Bd.wait(g, YSEM[i], ("x1d", LX1D[i][-1]))
                    Bd.op(lambda: g.dma_start(out=dbg["x1"], in_=x1b),
                          inc=("X", ("dbgx1",)))
                lastO = {}
                for oc in range(RBN, NDC):
                    lastO[FI_SLOT[oc]] = oc
                for slot, oc in lastO.items():
                    Bd.wait(g, ["O", "O2", "O3", "O4"][slot], ("out", oc))
                for oc in range(RBN):
                    Bd.wait(g, RB_SEM[oc], ("wout", oc))

            return dict(scalar=f_sc, vector=f_vec, sync=f_sync, gpsimd=f_gp)

        def ph_dbg_a(e):
            def f_gp(g):
                Bd.op(lambda: g.dma_start(out=dbg["q"], in_=Qt), inc=("X", ("dbgq",)))
                Bd.op(lambda: g.dma_start(out=dbg["v"], in_=Vt), inc=("X", ("dbgv",)))
                Bd.wait(g, "X", ("dbgv",))
            return dict(gpsimd=f_gp)

        def ph_bat(e):
            parts = [ph_lnqkv(e, 1), ph_att(e), ph_wo(e), ph_ln2(e), ph_gu(e)]
            names = ("sync", "gpsimd", "scalar", "vector", "tensor")
            out = {}
            for nme in names:
                fs = [p[nme] for p in parts if nme in p]
                if fs:
                    out[nme] = (lambda fs: lambda eng: [f(eng) for f in fs])(fs)
            return out

        def ph_a(e):
            parts = [ph_setup(e), ph_lnqkv(e, 0),
                     ph_lnqkv(e, 1), ph_att(e), ph_wo(e), ph_ln2(e), ph_gu(e)]
            names = ("sync", "gpsimd", "scalar", "vector", "tensor")
            out = {}
            for nme in names:
                fs = [p[nme] for p in parts if nme in p]
                if fs:
                    out[nme] = (lambda fs: lambda eng: [f(eng) for f in fs])(fs)
            return out

        phases = [ph_a]
        if debug:
            phases.append(ph_dbg_a)
        phases += [ph_dn, ph_fi]
        def ph_wog(e):
            parts = [ph_wo(e), ph_ln2(e), ph_gu(e)]
            names = ("sync", "gpsimd", "scalar", "vector", "tensor")
            out = {}
            for nme in names:
                fs = [p[nme] for p in parts if nme in p]
                if fs:
                    out[nme] = (lambda fs: lambda eng: [f(eng) for f in fs])(fs)
            return out


        Bd.mode = "count"
        for ph in phases:
            bodies = ph(None)
            for name in ("sync", "gpsimd", "scalar", "vector", "tensor"):
                if name in bodies:
                    bodies[name](None)

        Bd.mode = "emit"
        for ph in phases:
            bodies = ph(None)
            with nc.Block() as blk:
                if "sync" in bodies:
                    blk.sync(bodies["sync"])
                if "gpsimd" in bodies:
                    blk.gpsimd(bodies["gpsimd"])
                if "scalar" in bodies:
                    blk.scalar(bodies["scalar"])
                if "vector" in bodies:
                    blk.vector(bodies["vector"])
                if "tensor" in bodies:
                    blk.tensor(bodies["tensor"])

    return nc


def _blocks(W):
    Kin, Mout = W.shape
    nK, nM = Kin // 128, Mout // 128
    return np.ascontiguousarray(
        W.reshape(nK, 128, nM, 128).transpose(2, 1, 0, 3).reshape(nM, 128, nK * 128))


_CACHE = {}


def kernel(L0_attn_x, L0_attn_w_qkv, L0_attn_w_o, L0_ffn_gate_w, L0_ffn_up_w,
           L0_ffn_down_w, L0_ffn_gate_s, L0_ffn_up_s, L0_ffn_down_s):
    x = np.asarray(L0_attn_x, dtype=np.float32)
    w_qkv = np.asarray(L0_attn_w_qkv, dtype=np.float32)
    w_o = np.asarray(L0_attn_w_o, dtype=np.float32)
    gate_w = np.asarray(L0_ffn_gate_w, dtype=np.float32)
    up_w = np.asarray(L0_ffn_up_w, dtype=np.float32)
    down_w = np.asarray(L0_ffn_down_w, dtype=np.float32)
    gate_s = np.asarray(L0_ffn_gate_s, dtype=np.float32)
    up_s = np.asarray(L0_ffn_up_s, dtype=np.float32)
    down_s = np.asarray(L0_ffn_down_s, dtype=np.float32)

    if "nc" not in _CACHE:
        _CACHE["nc"] = build()
    nc = _CACHE["nc"]

    gw_b = _blocks(np.ascontiguousarray(gate_w.T))
    uw_b = _blocks(np.ascontiguousarray(up_w.T))
    dw_b = _blocks(np.ascontiguousarray(down_w.T))
    gs_r = np.ascontiguousarray(gate_s.reshape(NIC, 128).T)
    us_r = np.ascontiguousarray(up_s.reshape(NIC, 128).T)
    ds_r = np.ascontiguousarray(down_s.reshape(NDC, 128).T)

    wq_h = [_blocks(np.ascontiguousarray(w_qkv[:, hh * 1024:(hh + 1) * 1024]))
            for hh in (0, 1)]
    wk_h = [_blocks(np.ascontiguousarray(w_qkv[:, 2048 + hh * 1024:2048 + (hh + 1) * 1024]))
            for hh in (0, 1)]
    wv_full = w_qkv[:, 4096:6144]
    wv_h = []
    wo_h = []
    for hh in (0, 1):
        colperm = (np.concatenate([wv_full[:, 1024:], wv_full[:, :1024]], axis=1)
                   if hh else wv_full)
        wv_h.append(_blocks(np.ascontiguousarray(colperm)))
        rowperm = (np.concatenate([w_o[1024:, :], w_o[:1024, :]], axis=0)
                   if hh else w_o)
        wo_h.append(_blocks(np.ascontiguousarray(rowperm)))

    in_maps = []
    xT_c = {}
    for c in range(8):
        b, hh = c >> 1, c & 1
        key = (b, hh)
        if key not in xT_c:
            xt = x[b].T
            if hh:
                xt = np.concatenate([xt[:, 1024:], xt[:, :1024]], axis=1)
            xT_c[key] = np.ascontiguousarray(xt)
        in_maps.append(dict(
            xT=xT_c[key], WQ=wq_h[hh], WK=wk_h[hh], WV=wv_h[hh], WOw=wo_h[hh],
            GW=gw_b, UW=uw_b, DW=dw_b, GSs=gs_r, USs=us_r, DSs=ds_r))

    res = run_bass_kernel_spmd(nc, in_maps, core_ids=list(range(8)))

    out = np.empty((B, S, D), dtype=np.float32)
    for c in range(8):
        b, hh = c >> 1, c & 1
        out[b, hh * 1024:(hh + 1) * 1024, :] = res.results[c]["outT"].T
    return out
